# version 01
# baseline (speedup 1.0000x reference)
from contextlib import ExitStack
import numpy as np
import ml_dtypes
import concourse.bass as bass
import concourse.mybir as mybir
from concourse.bass_utils import run_bass_kernel_spmd

F32 = mybir.dt.float32
BF16 = mybir.dt.bfloat16
F16 = mybir.dt.float16
I8 = mybir.dt.int8
AF = mybir.ActivationFunctionType
ALU = mybir.AluOpType
AX = mybir.AxisListType

T = 1024
D = 4096
LCLAMP = 5.0
INV_D = 1.0 / 4096.0


def chain(eng, ctr, inst):
    ctr.inc(inst, 1)
    eng.wait_ge(ctr.sem, ctr.n)
    return inst


class Ctr:
    def __init__(self, sem):
        self.sem = sem
        self.n = 0

    def inc(self, inst, k=1):
        inst.then_inc(self.sem, k)
        self.n += k
        return self.n


def emit_phase_a(nc, es, blk, sem, dram, no_prev=False, stop_after='all'):
    PROJ = ["k", "v", "r", "g"]

    xn_fm = es.enter_context(nc.sbuf_tensor("xn_fm_sb", [128, 32 * 1024], BF16))
    xnprev = es.enter_context(nc.sbuf_tensor("xnprev", [128, 32], BF16))
    mixbuf = es.enter_context(nc.sbuf_tensor("mixbuf", [128, 32 * 1024], BF16))
    mixtmp = es.enter_context(nc.sbuf_tensor("mixtmp", [128, 1024], BF16))
    consts = es.enter_context(nc.sbuf_tensor("consts", [128, 4], F32))
    onesr = es.enter_context(nc.sbuf_tensor("onesr", [1, 128], F32))
    pstat = es.enter_context(nc.sbuf_tensor("pstat", [1, 8], F32))
    xprow = es.enter_context(nc.sbuf_tensor("xprow", [1, D], F32))
    xpcol = es.enter_context(nc.sbuf_tensor("xpcol", [128, 32], F32))
    mcol = {m: es.enter_context(nc.sbuf_tensor(f"mcol{m}", [128, 32], F32))
            for m in PROJ}
    m1col = {m: es.enter_context(nc.sbuf_tensor(f"m1col{m}", [128, 32], F32))
             for m in PROJ}

    NT = 8
    ROW0 = [tt * 128 for tt in range(NT)]

    def g_init(g):
        sem["sgi"].inc(g.memset(consts[:, 0:1], 1e-5), 1)
        sem["sgi"].inc(g.memset(onesr[:, :], 1.0), 1)
        for m in PROJ:
            sem["sini"].inc(g.dma_start(out=mcol[m][:, :], in_=dram[f"m{m}"][:, :]), 16)
        sem["sini"].inc(g.dma_start(out=xprow[:, :], in_=dram["x_prev_row"][:, :]), 16)
        sem["sini"].inc(g.dma_start(out=xpcol[:, :], in_=dram["x_prev_col"][:, :]), 16)
    blk.gpsimd(g_init)
    SINI = sem["sini"].n
    SGI = sem["sgi"].n

    def v_init(v):
        v.wait_ge(sem["sini"].sem, SINI)
        v.wait_ge(sem["sgi"].sem, SGI)
        for mi_, m in enumerate(PROJ):
            inst = v.tensor_scalar(m1col[m][:, :], mcol[m][:, :], -1.0, 1.0,
                                   ALU.mult, ALU.add)
            if mi_ == len(PROJ) - 1:
                sem["sv1"].inc(inst, 1)
    blk.vector(v_init)
    SV1_INIT = sem["sv1"].n

    with (
        nc.sbuf_tensor("lnx", [128, 2 * D], F32) as xbuf,
        nc.sbuf_tensor("lnsq", [128, D], BF16) as sqbuf,
        nc.sbuf_tensor("lnst", [128, 16], F32) as stat,
        nc.sbuf_tensor("lntm", [128, 2 * D], BF16) as xnbuf,
        nc.psum_tensor("pbc", [128, 2], F32) as pbc,
    ):
        def sy_ln(sy):
            for tt in range(NT):
                s = tt % 2
                if tt >= 2:
                    sy.wait_ge(sem["sacast"].sem, tt - 1)
                sem[f"sxl{s}"].inc(
                    sy.dma_start(out=xbuf[:, s * D:(s + 1) * D],
                                 in_=dram["x_sl"][ROW0[tt]:ROW0[tt] + 128, :]), 16)
                if tt >= 1:
                    pt = tt - 1
                    sp = pt % 2
                    sy.wait_ge(sem["sacast"].sem, pt + 1)
                    sem[f"stp{sp}"].inc(
                        sy.dma_start(
                            xn_fm[:, :].rearrange("p (db f) -> p db f", db=32)[
                                :, :, ROW0[pt]:ROW0[pt] + 128],
                            xnbuf[:, sp * D:(sp + 1) * D].rearrange(
                                "p (db f) -> p db f", db=32),
                            transpose=True), 16)
            pt = NT - 1
            sp = pt % 2
            sy.wait_ge(sem["sacast"].sem, pt + 1)
            sem[f"stp{sp}"].inc(
                sy.dma_start(
                    xn_fm[:, :].rearrange("p (db f) -> p db f", db=32)[
                        :, :, ROW0[pt]:ROW0[pt] + 128],
                    xnbuf[:, sp * D:(sp + 1) * D].rearrange(
                        "p (db f) -> p db f", db=32),
                    transpose=True), 16)
        blk.sync(sy_ln)
        STP_LN = (sem["stp0"].n, sem["stp1"].n)

        def v_ln(v):
            for tt in range(NT):
                s = tt % 2
                c = 8 * s
                v.wait_ge(sem[f"sxl{s}"].sem, 16 * (tt // 2 + 1))
                if tt >= 2:
                    v.wait_ge(sem["sacast"].sem, tt - 1)
                chain(v, sem["svs"],
                      v.reduce_sum(stat[:, c + 0:c + 1], xbuf[:, s * D:(s + 1) * D],
                                   axis=AX.X))
                v.wait_ge(sem["sasq"].sem, tt + 1)
                chain(v, sem["svs"],
                      v.tensor_scalar_mul(stat[:, c + 2:c + 3], stat[:, c + 0:c + 1],
                                          INV_D))
                chain(v, sem["svs"],
                      v.tensor_mul(stat[:, c + 3:c + 4], stat[:, c + 2:c + 3],
                                   stat[:, c + 2:c + 3]))
                sem["sv2"].inc(
                    v.scalar_tensor_tensor(stat[:, c + 4:c + 5], stat[:, c + 1:c + 2],
                                           INV_D, stat[:, c + 3:c + 4],
                                           ALU.mult, ALU.subtract), 1)
                v.wait_ge(sem["sasq2"].sem, tt + 1)
                chain(v, sem["svs"],
                      v.reciprocal(stat[:, c + 6:c + 7], stat[:, c + 5:c + 6]))
                chain(v, sem["svs"],
                      v.scalar_tensor_tensor(stat[:, c + 7:c + 8], stat[:, c + 2:c + 3],
                                             -1.0, stat[:, c + 6:c + 7],
                                             ALU.mult, ALU.mult))
                if tt >= 2:
                    sp2 = (tt - 2) % 2
                    v.wait_ge(sem[f"stp{sp2}"].sem, 16 * ((tt - 2) // 2 + 1))
                sem["sacast"].inc(
                    v.tensor_scalar(xnbuf[:, s * D:(s + 1) * D],
                                    xbuf[:, s * D:(s + 1) * D],
                                    stat[:, c + 6:c + 7], stat[:, c + 7:c + 8],
                                    ALU.mult, ALU.add), 1)
        blk.vector(v_ln)

        def a_ln(a):
            a.wait_ge(sem["sv1"].sem, SV1_INIT)
            a.wait_ge(sem["sgi"].sem, SGI)
            for tt in range(NT):
                s = tt % 2
                c = 8 * s
                a.wait_ge(sem[f"sxl{s}"].sem, 16 * (tt // 2 + 1))
                sem["sasq"].inc(
                    a.activation(sqbuf[:, :], xbuf[:, s * D:(s + 1) * D], AF.Square,
                                 accum_out=stat[:, c + 1:c + 2]), 1)
                a.wait_ge(sem["sv2"].sem, tt + 1)
                sem["sasq2"].inc(
                    a.activation(stat[:, c + 5:c + 6], stat[:, c + 4:c + 5], AF.Sqrt,
                                 bias=consts[:, 0:1]), 1)
        blk.scalar(a_ln)

        if no_prev:
            def g_noprev(g):
                sem["svpv"].inc(g.memset(xnprev[:, :], 0.0), 3)
            blk.gpsimd(g_noprev)
            sem["saprev"].n += 0
        def v_lnprev(v):
            v.wait_ge(sem["sini"].sem, SINI)
            v.wait_ge(sem["sgi"].sem, SGI)
            chain(v, sem["svs"],
                  v.reduce_sum(pstat[0:1, 0:1], xprow[0:1, :], axis=AX.X))
            v.wait_ge(sem["saprev"].sem, 1)
            chain(v, sem["svs"],
                  v.tensor_scalar_mul(pstat[0:1, 2:3], pstat[0:1, 0:1], INV_D))
            chain(v, sem["svs"],
                  v.tensor_mul(pstat[0:1, 3:4], pstat[0:1, 2:3], pstat[0:1, 2:3]))
            sem["svpv"].inc(
                v.scalar_tensor_tensor(pstat[0:1, 4:5], pstat[0:1, 1:2], INV_D,
                                       pstat[0:1, 3:4], ALU.mult, ALU.subtract), 1)
            v.wait_ge(sem["saprev"].sem, 2)
            chain(v, sem["svs"],
                  v.reciprocal(pstat[0:1, 6:7], pstat[0:1, 5:6]))
            sem["svpv"].inc(
                v.scalar_tensor_tensor(pstat[0:1, 7:8], pstat[0:1, 2:3], -1.0,
                                       pstat[0:1, 6:7], ALU.mult, ALU.mult), 1)
        if not no_prev:
            blk.vector(v_lnprev)

        def a_lnprev(a):
            a.wait_ge(sem["sasq"].sem, NT)
            sem["saprev"].inc(
                a.activation(sqbuf[0:1, :], xprow[0:1, :], AF.Square,
                             accum_out=pstat[0:1, 1:2]), 1)
            a.wait_ge(sem["svpv"].sem, 1)
            sem["saprev"].inc(
                a.activation(pstat[0:1, 5:6], pstat[0:1, 4:5], AF.Sqrt,
                             bias=consts[0:1, 0:1]), 1)
        if not no_prev:
            blk.scalar(a_lnprev)

        def t_lnprev(t):
            t.wait_ge(sem["svpv"].sem, 2)
            sem["smpv"].inc(
                t.matmul(pbc[:, 0:2], onesr[0:1, :], pstat[0:1, 6:8],
                         start=True, stop=True), 1)
        if not no_prev:
            blk.tensor(t_lnprev)

        def v_lnprev2(v):
            v.wait_ge(sem["smpv"].sem, 1)
            sem["svpv"].inc(
                v.scalar_tensor_tensor(
                    xnprev[:, :], xpcol[:, :], pbc[:, 0:1],
                    pbc[:, 1:2].broadcast_to([128, 32]),
                    ALU.mult, ALU.add), 1)
        if not no_prev:
            blk.vector(v_lnprev2)
        SVPV = sem["svpv"].n

        def a_xnout(a):
            a.wait_ge(sem["stp0"].sem, STP_LN[0])
            a.wait_ge(sem["stp1"].sem, STP_LN[1])
            for db in range(32):
                sem["sxn"].inc(
                    a.dma_start(out=dram["xn_fm"][db * 128:(db + 1) * 128, :],
                                in_=xn_fm[:, db * 1024:(db + 1) * 1024]), 16)
        blk.scalar(a_xnout)

    NSLAB = 4
    slab = es.enter_context(nc.sbuf_tensor("slab", [128, NSLAB * 512], BF16))
    stage = es.enter_context(nc.sbuf_tensor("stage", [128, 2 * 4096], BF16))
    ktms = es.enter_context(nc.sbuf_tensor("ktms", [128, 2 * 4096], BF16))
    psum = es.enter_context(nc.psum_tensor("ps", [128, 8 * 512], F32))

    STORE_CNT = {0: 0}
    KTP_CNT = {0: 0}
    nstp = [0, 0]
    nktp = [0, 0]
    for ogc in range(1, 33):
        m = PROJ[(ogc - 1) // 8]
        par = (ogc - 1) % 2
        nstp[par] += 8 * 16 if m == "v" else 4 * 16
        if m == "k":
            nktp[par] += 12 * 16
        STORE_CNT[ogc] = nstp[par]
        KTP_CNT[ogc] = nktp[par]

    if stop_after == 'ln':
        def sy_fin(sy):
            sy.wait_ge(sem["sxn"].sem, sem["sxn"].n)
        blk.sync(sy_fin)
        return
    def v_mix_evac(v):
        v.wait_ge(sem["stp0"].sem, STP_LN[0])
        v.wait_ge(sem["stp1"].sem, STP_LN[1])
        v.wait_ge(sem["svpv"].sem, SVPV)
        for mi, m in enumerate(PROJ):
            if mi > 0:
                v.wait_ge(sem["ssf"].sem, mi * 256)
            for db in range(32):
                xmain = xn_fm[:, db * 1024:(db + 1) * 1024]
                chain(v, sem["svs"],
                      v.tensor_scalar_mul(mixtmp[:, :], xmain, mcol[m][:, db:db + 1]))
                v.scalar_tensor_tensor(
                    mixbuf[:, db * 1024 + 1:(db + 1) * 1024],
                    xn_fm[:, db * 1024:db * 1024 + 1023],
                    m1col[m][:, db:db + 1], mixtmp[:, 1:1024],
                    ALU.mult, ALU.add)
                sem["svmix"].inc(
                    v.scalar_tensor_tensor(
                        mixbuf[:, db * 1024:db * 1024 + 1],
                        xnprev[:, db:db + 1],
                        m1col[m][:, db:db + 1], mixtmp[:, 0:1],
                        ALU.mult, ALU.add), 1)
            if m == "g":
                continue
            for og in range(8):
                ogc = mi * 8 + og + 1
                par = (ogc - 1) % 2
                v.wait_ge(sem["ssf"].sem, 32 * ogc)
                if ogc > 2:
                    par2 = (ogc - 3) % 2
                    v.wait_ge(sem[f"sst{par2}"].sem, STORE_CNT[ogc - 2])
                    if KTP_CNT[ogc - 2] > 0:
                        v.wait_ge(sem[f"sktp{par2}"].sem, KTP_CNT[ogc - 2])
                for bk in range(8):
                    inst = v.tensor_copy(
                        stage[:, par * 4096 + bk * 512:par * 4096 + (bk + 1) * 512],
                        psum[:, bk * 512:(bk + 1) * 512])
                    if bk == 7:
                        sem["sev"].inc(inst, 1)
    blk.vector(v_mix_evac)

    def g_slabs(g):
        g.wait_ge(sem["stp0"].sem, STP_LN[0])
        g.wait_ge(sem["stp1"].sem, STP_LN[1])
        si = 0
        for mi, m in enumerate(PROJ):
            for og in range(8):
                for i in range(32):
                    slot = si % NSLAB
                    if si >= NSLAB:
                        g.wait_ge(sem["ssf"].sem, si - NSLAB + 1)
                    sem[f"swl{slot}"].inc(
                        g.dma_start(
                            out=slab[:, slot * 512:(slot + 1) * 512],
                            in_=dram[f"w{m}T"][i * 128:(i + 1) * 128,
                                               og * 512:(og + 1) * 512]), 16)
                    si += 1
    blk.gpsimd(g_slabs)

    def t_proj(t):
        si = 0
        ogc = 0
        for mi, m in enumerate(PROJ):
            for og in range(8):
                if ogc >= 1:
                    t.wait_ge(sem["sev"].sem, ogc)
                for i in range(32):
                    slot = si % NSLAB
                    t.wait_ge(sem[f"swl{slot}"].sem, 16 * (si // NSLAB + 1))
                    t.wait_ge(sem["svmix"].sem, mi * 32 + i + 1)
                    last_i = (i == 31)
                    if m == "v":
                        for tt in range(8):
                            inst = t.matmul(
                                psum[:, tt * 512:(tt + 1) * 512],
                                mixbuf[:, i * 1024 + tt * 128:i * 1024 + (tt + 1) * 128],
                                slab[:, slot * 512:(slot + 1) * 512],
                                start=(i == 0), stop=last_i)
                            if tt == 7:
                                sem["ssf"].inc(inst, 1)
                    else:
                        for ot in range(4):
                            for tt in range(2):
                                inst = t.matmul(
                                    psum[:, (ot * 2 + tt) * 512:(ot * 2 + tt + 1) * 512],
                                    slab[:, slot * 512 + ot * 128:slot * 512 + (ot + 1) * 128],
                                    mixbuf[:, i * 1024 + tt * 512:i * 1024 + (tt + 1) * 512],
                                    start=(i == 0), stop=last_i)
                                if ot == 3 and tt == 1:
                                    sem["ssf"].inc(inst, 1)
                    si += 1
                ogc += 1
    blk.tensor(t_proj)

    def a_evac(a):
        ogc = 0
        for mi, m in enumerate(PROJ):
            for og in range(8):
                ogc += 1
                if m != "g":
                    continue
                par = (ogc - 1) % 2
                a.wait_ge(sem["ssf"].sem, 32 * ogc)
                if ogc > 2:
                    par2 = (ogc - 3) % 2
                    a.wait_ge(sem[f"sst{par2}"].sem, STORE_CNT[ogc - 2])
                    if KTP_CNT[ogc - 2] > 0:
                        a.wait_ge(sem[f"sktp{par2}"].sem, KTP_CNT[ogc - 2])
                for bk in range(8):
                    inst = a.activation(
                        stage[:, par * 4096 + bk * 512:par * 4096 + (bk + 1) * 512],
                        psum[:, bk * 512:(bk + 1) * 512], AF.Silu)
                    if bk == 7:
                        sem["sev"].inc(inst, 1)
    blk.scalar(a_evac)

    def sy_store(sy):
        ogc = 0
        for mi, m in enumerate(PROJ):
            for og in range(8):
                ogc += 1
                par = (ogc - 1) % 2
                sy.wait_ge(sem["sev"].sem, ogc)
                if m == "v":
                    for tt in range(8):
                        sem[f"sst{par}"].inc(
                            sy.dma_start(
                                out=dram["v_tm"][tt * 128:(tt + 1) * 128,
                                                 og * 512:(og + 1) * 512],
                                in_=stage[:, par * 4096 + tt * 512:par * 4096 + (tt + 1) * 512]), 16)
                else:
                    dd = dram[f"{m}_fm"]
                    for ot in range(4):
                        sem[f"sst{par}"].inc(
                            sy.dma_start(
                                out=dd[(og * 4 + ot) * 128:(og * 4 + ot + 1) * 128, :],
                                in_=stage[:, par * 4096 + ot * 1024:par * 4096 + (ot + 1) * 1024]), 16)
                if m == "k":
                    for ot in range(4):
                        sem[f"sktp{par}"].inc(
                            sy.dma_start(
                                ktms[:, par * 4096:(par + 1) * 4096].rearrange(
                                    "p (tb f) -> p tb f", tb=8)[:, :, ot * 128:(ot + 1) * 128],
                                stage[:, par * 4096 + ot * 1024:par * 4096 + (ot + 1) * 1024]
                                .rearrange("p (tb f) -> p tb f", tb=8),
                                transpose=True), 16)
                    sy.wait_ge(sem[f"sktp{par}"].sem, sem[f"sktp{par}"].n)
                    for tb in range(8):
                        sem[f"sktp{par}"].inc(
                            sy.dma_start(
                                out=dram["k_tm"][tb * 128:(tb + 1) * 128,
                                                 og * 512:(og + 1) * 512],
                                in_=ktms[:, par * 4096 + tb * 512:par * 4096 + (tb + 1) * 512]), 16)
        sy.wait_ge(sem["sxn"].sem, sem["sxn"].n)
        for pp in range(2):
            sy.wait_ge(sem[f"sst{pp}"].sem, sem[f"sst{pp}"].n)
            sy.wait_ge(sem[f"sktp{pp}"].sem, sem[f"sktp{pp}"].n)
    blk.sync(sy_store)


PHASE_A_SEMS = ["sini", "sgi", "sxl0", "sxl1", "sv1", "sv2", "svs", "sasq", "sasq2",
                "sacast", "stp0", "stp1", "svmix", "swl0", "swl1", "swl2", "swl3",
                "ssf", "sev", "sxn", "sst0", "sst1", "sktp0", "sktp1",
                "saprev", "svpv", "smpv"]


def build_phase_a(**kw):
    nc = bass.Bass("TRN2", target_bir_lowering=False)
    dram = {}
    dram["x_sl"] = nc.dram_tensor("x_sl", [T, D], F32, kind="ExternalInput").ap()
    dram["x_prev_row"] = nc.dram_tensor("x_prev_row", [1, D], F32, kind="ExternalInput").ap()
    dram["x_prev_col"] = nc.dram_tensor("x_prev_col", [128, 32], F32, kind="ExternalInput").ap()
    for m in "kvrg":
        dram[f"w{m}T"] = nc.dram_tensor(f"w{m}T", [D, D], F32, kind="ExternalInput").ap()
        dram[f"m{m}"] = nc.dram_tensor(f"m{m}", [128, 32], F32, kind="ExternalInput").ap()
    for m in "krg":
        dram[f"{m}_fm"] = nc.dram_tensor(f"{m}_fm", [D, T], BF16, kind="ExternalOutput").ap()
    dram["xn_fm"] = nc.dram_tensor("xn_fm", [D, T], BF16, kind="ExternalOutput").ap()
    dram["v_tm"] = nc.dram_tensor("v_tm", [T, D], BF16, kind="ExternalOutput").ap()
    dram["k_tm"] = nc.dram_tensor("k_tm", [T, D], BF16, kind="ExternalOutput").ap()
    es = ExitStack()
    with es:
        sem = {n: Ctr(es.enter_context(nc.semaphore(n))) for n in PHASE_A_SEMS}
        blk = es.enter_context(nc.Block())
        emit_phase_a(nc, es, blk, sem, dram, **kw)
    return nc


def host_inputs_phase_a(core, inp):
    b, p = core // 4, core % 4
    t0 = p * T
    x = np.asarray(inp["L0_attn_hidden"])
    x_sl = np.ascontiguousarray(x[b, t0:t0 + T])
    if p == 0:
        xprev = np.zeros((1, D), np.float32)
    else:
        xprev = np.ascontiguousarray(x[b, t0 - 1:t0])
    d = {
        "x_sl": x_sl,
        "x_prev_row": xprev,
        "x_prev_col": np.ascontiguousarray(xprev.reshape(32, 128).T),
        "wkT": np.ascontiguousarray(np.asarray(inp["L0_attn_w_key"]).T),
        "wvT": np.ascontiguousarray(np.asarray(inp["L0_attn_w_value"]).T),
        "wrT": np.ascontiguousarray(np.asarray(inp["L0_attn_w_receptance"]).T),
        "wgT": np.ascontiguousarray(np.asarray(inp["L0_attn_w_gate"]).T),
        "mk": np.ascontiguousarray(
            np.asarray(inp["L0_attn_time_mix_key"]).reshape(32, 128).T),
        "mv": np.ascontiguousarray(
            np.asarray(inp["L0_attn_time_mix_value"]).reshape(32, 128).T),
        "mr": np.ascontiguousarray(
            np.asarray(inp["L0_attn_time_mix_receptance"]).reshape(32, 128).T),
        "mg": np.ascontiguousarray(
            np.asarray(inp["L0_attn_time_mix_gate"]).reshape(32, 128).T),
    }
    return d


def cinc(eng, ctr, inst, k=1):
    ctr.inc(inst, k)
    eng.wait_ge(ctr.sem, ctr.n)
    return inst

C = 32
NH = 16
HTOK = NH * 1024

SEMS_B = ["gms", "gld", "gld2", "sv", "svs", "sa", "shl0", "shl1", "svz",
          "ss0m", "ss0e", "ss0t", "svgp", "svp", "smA", "sAe", "smO", "sOe",
          "smS", "sSr", "sso", "smD", "sdkb", "sadt", "svlr", "sgs"]


def emit_phase_b(nc, es, blk, sem, dram, nch=32, do_s0=True, do_dkb=True, do_dt=True, do_cinit=True, ndt=16, dbg=0):
    kfm = es.enter_context(nc.sbuf_tensor("kfm", [128, HTOK], BF16))
    rfm = es.enter_context(nc.sbuf_tensor("rfm", [128, HTOK], BF16))
    vtm = es.enter_context(nc.sbuf_tensor("vtm", [128, 8 * 2048], BF16))
    ktm = es.enter_context(nc.sbuf_tensor("ktm", [128, 8 * 2048], BF16))
    outsb = es.enter_context(nc.sbuf_tensor("outsb", [128, HTOK], BF16))
    z0 = es.enter_context(nc.sbuf_tensor("z0", [128, 1024], BF16))
    kbar = es.enter_context(nc.sbuf_tensor("kbar", [128, 2048], BF16))
    vpb = es.enter_context(nc.sbuf_tensor("vpb", [128, 2048], BF16))
    rt_c = es.enter_context(nc.sbuf_tensor("rt_c", [128, 2 * 512], BF16))
    rh_c = es.enter_context(nc.sbuf_tensor("rh_c", [128, 2 * 512], BF16))
    kh_c = es.enter_context(nc.sbuf_tensor("kh_c", [128, 2 * 512], BF16))
    Asb = es.enter_context(nc.sbuf_tensor("Asb", [128, 1024], BF16))
    Ssb = es.enter_context(nc.sbuf_tensor("Ssb", [128, 2 * 2048], BF16))
    s0sb = es.enter_context(nc.sbuf_tensor("s0sb", [32, 1024], BF16))
    s0tm = es.enter_context(nc.sbuf_tensor("s0tm", [128, 8 * 32], BF16))
    Lfm = es.enter_context(nc.sbuf_tensor("Lfm", [128, 32], F32))
    nLfm = es.enter_context(nc.sbuf_tensor("nLfm", [128, 32], F32))
    b155 = es.enter_context(nc.sbuf_tensor("b155", [128, 32], F32))
    b145 = es.enter_context(nc.sbuf_tensor("b145", [128, 32], F32))
    Ls = es.enter_context(nc.sbuf_tensor("Ls", [128, 32], F32))
    d32s = es.enter_context(nc.sbuf_tensor("d32s", [128, 32], BF16))
    Drt = es.enter_context(nc.sbuf_tensor("Drt", [128, 512], BF16))
    Drh = es.enter_context(nc.sbuf_tensor("Drh", [128, 512], BF16))
    Dkh = es.enter_context(nc.sbuf_tensor("Dkh", [128, 512], BF16))
    Dkb = es.enter_context(nc.sbuf_tensor("Dkb", [128, 2048], BF16))
    Lrow = es.enter_context(nc.sbuf_tensor("Lrow", [1, 2048], F32))
    nrow = es.enter_context(nc.sbuf_tensor("nrow", [1, 128], F32))
    tfc = es.enter_context(nc.sbuf_tensor("tfc", [128, 32], F32))
    iot = es.enter_context(nc.sbuf_tensor("iot", [128, 32], F32))
    zcol = es.enter_context(nc.sbuf_tensor("zcol", [128, 1], F32))
    Ept = es.enter_context(nc.sbuf_tensor("Ept", [128, 16 * 32], BF16))
    I32p = es.enter_context(nc.sbuf_tensor("I32p", [128, 64], BF16))
    m32u = es.enter_context(nc.sbuf_tensor("m32u", [32, 1024], I8))
    zer = es.enter_context(nc.sbuf_tensor("zer", [32, 8], BF16))
    ps_out = es.enter_context(nc.psum_tensor("ps_out", [128, 1024], F32))
    ps_A = es.enter_context(nc.psum_tensor("ps_A", [128, 1024], F32))
    ps_S = es.enter_context(nc.psum_tensor("ps_S", [128, 2048], F32))

    def g_init(g):
        sem["gms"].inc(g.memset(zer[:, :], 0.0), 1)
        sem["gms"].inc(g.memset(zcol[:, :], 0.0), 1)
        chain(g, sem["sgs"], g.memset(I32p[:, :], 1.0))
        sem["gms"].inc(g.affine_select(I32p[0:32, :], I32p[0:32, :], [[0, 2], [1, 32]],
                                       ALU.is_equal, 0.0, channel_multiplier=-1), 1)
        chain(g, sem["sgs"], g.memset(m32u[:, :], 1))
        sem["gms"].inc(g.affine_select(m32u[:, :], m32u[:, :], [[0, 32], [-1, 32]],
                                       ALU.is_ge, 0, channel_multiplier=1), 1)
        sem["gms"].inc(g.iota(iot[:, :], [[1, 32]], channel_multiplier=0, allow_small_or_imprecise_dtypes=True), 1)
        sem["gms"].inc(g.iota(nrow[0:1, :], [[0, 4], [1, 32]], channel_multiplier=0, allow_small_or_imprecise_dtypes=True), 1)
        chain(g, sem["sgs"], g.memset(Ept[:, :], 0.0))
        for q in range(16):
            sem["gms"].inc(g.memset(Ept[0:64, q * 32 + 2 * q:q * 32 + 2 * q + 1], 1.0), 1)
            sem["gms"].inc(g.memset(Ept[64:128, q * 32 + 2 * q + 1:q * 32 + 2 * q + 2], 1.0), 1)
        sem["gld"].inc(g.dma_start(out=Lfm[:, :], in_=dram["td_fm"][:, :]), 16)
        sem["gld"].inc(g.dma_start(out=tfc[:, :], in_=dram["tf_fm"][:, :]), 16)
        sem["gld"].inc(g.dma_start(out=Ls[:, :], in_=dram["td_s"][:, :]), 16)

        g.wait_ge(sem["gms"].sem, 2)
        for b0 in (32, 64, 96):
            sem["gld2"].inc(g.dma_start(out=I32p[b0:b0 + 32, :], in_=I32p[0:32, :]), 16)
        for ha in range(2):
            if ha == 1:
                g.wait_ge(sem["sso"].sem, 17 * 16)
            shl = sem[f"shl{ha}"]
            shl.inc(g.dma_start(out=Lrow[:, :],
                                in_=dram["td_row"][:, ha * 2048:(ha + 1) * 2048]), 16)
            shl.inc(g.dma_start(out=Ssb[:, 2048:4096],
                                in_=dram["s_in"][:, ha * 2048:(ha + 1) * 2048]), 16)
            for q in range(16):
                shl.inc(g.dma_start(
                    out=kfm[:, q * 1024:(q + 1) * 1024],
                    in_=dram["k_fm"][(ha * 16 + q) * 128:(ha * 16 + q + 1) * 128, :]), 16)
                shl.inc(g.dma_start(
                    out=rfm[:, q * 1024:(q + 1) * 1024],
                    in_=dram["r_fm"][(ha * 16 + q) * 128:(ha * 16 + q + 1) * 128, :]), 16)
            for tb in range(8):
                shl.inc(g.dma_start(
                    out=vtm[:, tb * 2048:(tb + 1) * 2048],
                    in_=dram["v_tm"][tb * 128:(tb + 1) * 128,
                                     ha * 2048:(ha + 1) * 2048]), 16)
                shl.inc(g.dma_start(
                    out=ktm[:, tb * 2048:(tb + 1) * 2048],
                    in_=dram["k_tm"][tb * 128:(tb + 1) * 128,
                                     ha * 2048:(ha + 1) * 2048]), 16)
    blk.gpsimd(g_init)
    GMS, GLD, GLD2 = sem["gms"].n, sem["gld"].n, sem["gld2"].n
    SHL = (sem["shl0"].n, sem["shl1"].n)

    def v_cinit(v):
        v.wait_ge(sem["sa"].sem, 1)
        chain(v, sem["svs"], v.tensor_scalar_min(Lfm[:, :], Lfm[:, :], LCLAMP))
        chain(v, sem["svs"], v.tensor_scalar_mul(nLfm[:, :], Lfm[:, :], -1.0))
        chain(v, sem["svs"], v.tensor_scalar_mul(b155[:, :], Lfm[:, :], 15.5))
        cinc(v, sem["sv"], v.tensor_scalar_mul(b145[:, :], Lfm[:, :], -14.5))
        v.wait_ge(sem["sa"].sem, 2)
        chain(v, sem["svs"], v.tensor_scalar_min(Ls[:, :], Ls[:, :], LCLAMP))
        cinc(v, sem["sv"], v.tensor_scalar_mul(Ls[:, :], Ls[:, :], -32.0))
        v.wait_ge(sem["sa"].sem, 3)
        cinc(v, sem["sv"], v.tensor_scalar_add(nrow[0:1, :], nrow[0:1, :], -31.0))
    if do_cinit:
        blk.vector(v_cinit)

    def a_cinit(a):
        a.wait_ge(sem["gld"].sem, GLD)
        a.wait_ge(sem["gms"].sem, GMS)
        sem["sa"].inc(a.activation(Lfm[:, :], Lfm[:, :], AF.Exp), 1)
        a.wait_ge(sem["sv"].sem, 1)
        sem["sa"].inc(a.activation(Ls[:, :], Ls[:, :], AF.Exp), 1)
        sem["sa"].inc(a.activation(Lfm[:, 0:1], Lfm[:, 0:1], AF.Copy), 1)
        a.wait_ge(sem["sv"].sem, 2)
        sem["sa"].inc(a.activation(d32s[:, :], Ls[:, :], AF.Exp), 1)
    if do_cinit:
        blk.scalar(a_cinit)
    SA_D = sem["sa"].n

    for ha in range(2):
        SHLn = SHL[ha]
        shl = sem[f"shl{ha}"]
        CI0 = ha * 32

        def a_dt(a, ha=ha, SHLn=SHLn, shl=shl):
            if do_cinit:
                a.wait_ge(sem["sv"].sem, 3)
            a.wait_ge(shl.sem, SHLn)
            if ha == 1:
                a.wait_ge(sem["svp"].sem, sem["svp"].n)
            for ptl in range(ndt):
                pt = ha * 16 + ptl
                a.activation(Drt[:, ptl * 32:(ptl + 1) * 32], iot[:, :], AF.Exp,
                             scale=nLfm[:, pt:pt + 1], bias=zcol[:, 0:1])
                a.activation(Drh[:, ptl * 32:(ptl + 1) * 32], iot[:, :], AF.Exp,
                             scale=nLfm[:, pt:pt + 1], bias=b155[:, pt:pt + 1])
                inst = a.activation(Dkh[:, ptl * 32:(ptl + 1) * 32], iot[:, :], AF.Exp,
                                    scale=Lfm[:, pt:pt + 1], bias=b145[:, pt:pt + 1])
            sem["sadt"].inc(inst, 1)
            inst = a.activation(Lrow[0:1, :], Lrow[0:1, :], AF.Exp)
            sem["sadt"].inc(inst, 1)
        if do_dt:
            blk.scalar(a_dt)

        def v_lrow(v, ha=ha):
            v.wait_ge(sem["sadt"].sem, 2 * ha + 2)
            cinc(v, sem["svlr"], v.tensor_scalar_min(Lrow[0:1, :], Lrow[0:1, :], LCLAMP))
        if do_dkb:
            blk.vector(v_lrow)

        def t_dkb(t, ha=ha):
            t.wait_ge(sem["sv"].sem, 3)
            t.wait_ge(sem["svlr"].sem, ha + 1)
            if ha == 1:
                t.wait_ge(sem["sSr"].sem, sem["sSr"].n)
            for nb in range(4):
                inst = t.matmul(
                    ps_S[:, nb * 512:(nb + 1) * 512], nrow[0:1, :],
                    Lrow[0:1, nb * 512:(nb + 1) * 512],
                    start=True, stop=True)
            sem["smD"].inc(inst, 1)
        if do_dkb:
            blk.tensor(t_dkb)

        def a_dkb(a, ha=ha):
            a.wait_ge(sem["smD"].sem, ha + 1)
            sem["sdkb"].inc(a.activation(Dkb[:, :], ps_S[:, :], AF.Exp), 1)
        if do_dkb:
            blk.scalar(a_dkb)

        def v_z0(v, ha=ha, SHLn=SHLn, shl=shl):
            v.wait_ge(shl.sem, SHLn)
            v.wait_ge(sem["sa"].sem, SA_D)
            if ha == 1:
                v.wait_ge(sem["ss0t"].sem, 16)
            for pt in range(16):
                zslot = 0
                if pt >= 1:
                    v.wait_ge(sem["ss0m"].sem, ha * 16 + pt)
                if dbg & 512:
                    sem["svz"].inc(
                        chain(v, sem["svs"],
                              v.tensor_mul(z0[:, 0:1024], rfm[:, 0:1024],
                                           kfm[:, 0:1024])), 0) if False else None
                    cinc(v, sem["svz"],
                         v.tensor_mul(z0[:, 0:1024], rfm[:, 0:1024], kfm[:, 0:1024]))
                else:
                    chain(v, sem["svs"],
                          v.tensor_mul(z0[:, zslot * 1024:(zslot + 1) * 1024],
                                       rfm[:, pt * 1024:(pt + 1) * 1024],
                                       kfm[:, pt * 1024:(pt + 1) * 1024]))
                    sem["svz"].inc(
                        v.tensor_scalar_mul(z0[:, zslot * 1024:(zslot + 1) * 1024],
                                            z0[:, zslot * 1024:(zslot + 1) * 1024],
                                            tfc[:, ha * 16 + pt:ha * 16 + pt + 1]), 1)
        if do_s0:
            blk.vector(v_z0)

        def t_s0(t, ha=ha):
            t.wait_ge(sem["gms"].sem, GMS)
            if ha == 1:
                t.wait_ge(sem["sAe"].sem, sem["sAe"].n)
            if dbg & 256:
                for pt in range(16):
                    t.wait_ge(sem["svz"].sem, ha * 16 + pt + 1)
                    inst = t.matmul(
                        ps_A[0:32, 0:512], Ept[:, 0:32], z0[:, 0:512],
                        start=True, stop=True)
                    sem["ss0m"].inc(inst, 1)
            else:
                for pt in range(16):
                    zslot = 0
                    t.wait_ge(sem["svz"].sem, ha * 16 + pt + 1)
                    for nb in range(2):
                        inst = t.matmul(
                            ps_A[0:32, nb * 512:(nb + 1) * 512],
                            Ept[:, pt * 32:(pt + 1) * 32],
                            z0[:, zslot * 1024 + nb * 512:zslot * 1024 + (nb + 1) * 512],
                            start=(pt == 0), stop=(pt == 15))
                    sem["ss0m"].inc(inst, 1)
        if do_s0:
            blk.tensor(t_s0)

        def v_s0e(v, ha=ha):
            v.wait_ge(sem["ss0m"].sem, 16 * (ha + 1))
            sem["ss0e"].inc(v.tensor_copy(s0sb[:, :], ps_A[0:32, :]), 1)
        if do_s0:
            blk.vector(v_s0e)

        def sy_s0t(sy, ha=ha):
            sy.wait_ge(sem["ss0e"].sem, ha + 1)
            if ha == 1:
                sy.wait_ge(sem["svgp"].sem, sem["svgp"].n)
            sem["ss0t"].inc(
                sy.dma_start(
                    s0tm[:, :].rearrange("p (tb f) -> p tb f", tb=8),
                    s0sb[:, :].rearrange("p (tb f) -> p tb f", tb=8),
                    transpose=True), 16)
        if do_s0:
            blk.sync(sy_s0t)

        def v_chunks(v, ha=ha, CI0=CI0):
            if do_s0:
                v.wait_ge(sem["ss0t"].sem, 16 * (ha + 1))
            if do_dkb:
                v.wait_ge(sem["sdkb"].sem, ha + 1)
            if do_dt:
                v.wait_ge(sem["sadt"].sem, 2 * ha + 1)
            if ha == 1:
                v.wait_ge(sem["sso"].sem, 17 * 16)
            for cc in range(nch):
                ci = CI0 + cc
                tb = cc // 4
                band = 32 * (cc % 4)
                slot = cc % 2
                gslot = tb % 2
                if cc % 4 == 0:
                    if ci >= 4:
                        v.wait_ge(sem["smO"].sem, ci)
                    chain(v, sem["svs"],
                          v.tensor_mul(kbar[:, :],
                                       ktm[:, tb * 2048:(tb + 1) * 2048], Dkb[:, :]))
                    if dbg & 16:
                        sem["svgp"].inc(
                            v.tensor_copy(vpb[:, :],
                                          vtm[:, tb * 2048:(tb + 1) * 2048]), 1)
                    else:
                        sem["svgp"].inc(
                            v.tensor_mul(
                                vpb[:, :].rearrange("p (h f) -> p h f", h=32),
                                vtm[:, tb * 2048:(tb + 1) * 2048]
                                .rearrange("p (h f) -> p h f", h=32),
                                s0tm[:, tb * 32:(tb + 1) * 32].unsqueeze(2)
                                .broadcast_to([128, 32, 64])), 1)
                if ci >= 2:
                    v.wait_ge(sem["smO"].sem, ci - 1)
                if dbg & 1:
                    chain(v, sem["svs"],
                          v.tensor_mul(rt_c[:, slot * 512:(slot + 1) * 512],
                                       rfm[:, 0:512], Drt[:, 0:512]))
                    chain(v, sem["svs"],
                          v.tensor_mul(rh_c[:, slot * 512:(slot + 1) * 512],
                                       rfm[:, 0:512], Drh[:, 0:512]))
                    sem["svp"].inc(
                        v.tensor_mul(kh_c[:, slot * 512:(slot + 1) * 512],
                                     kfm[:, 0:512], Dkh[:, 0:512]), 1)
                else:
                    src = rfm[:, :].rearrange("p (q f) -> p q f", q=16)[
                        :, :, cc * 32:(cc + 1) * 32]
                    ksrc = kfm[:, :].rearrange("p (q f) -> p q f", q=16)[
                        :, :, cc * 32:(cc + 1) * 32]
                    dsl = slice(0, 512)
                    chain(v, sem["svs"],
                          v.tensor_mul(rt_c[:, slot * 512:(slot + 1) * 512]
                                       .rearrange("p (q f) -> p q f", q=16),
                                       src, Drt[:, dsl].rearrange("p (q f) -> p q f", q=16)))
                    chain(v, sem["svs"],
                          v.tensor_mul(rh_c[:, slot * 512:(slot + 1) * 512]
                                       .rearrange("p (q f) -> p q f", q=16),
                                       src, Drh[:, dsl].rearrange("p (q f) -> p q f", q=16)))
                    sem["svp"].inc(
                        v.tensor_mul(kh_c[:, slot * 512:(slot + 1) * 512]
                                     .rearrange("p (q f) -> p q f", q=16),
                                     ksrc, Dkh[:, dsl].rearrange("p (q f) -> p q f", q=16)), 1)
                v.wait_ge(sem["smA"].sem, ci + 1)
                if ci >= 4:
                    v.wait_ge(sem["smO"].sem, ci - 3)
                if dbg & 2:
                    sem["sAe"].inc(
                        v.tensor_copy(Asb[band:band + 32, :],
                                      ps_A[band:band + 32, :]), 1)
                else:
                    for hh2 in range(2):
                        chain(v, sem["svs"], v.tensor_copy(
                            Asb[band:band + 32, :].rearrange(
                                "p (q two f) -> p q two f", q=16, two=2)[:, :, hh2, :],
                            ps_A[band:band + 32,
                                 hh2 * 512:(hh2 + 1) * 512].rearrange(
                                "p (q f) -> p q f", q=16)))
                    sem["sAe"].inc(
                        v.copy_predicated(Asb[band:band + 32, :], m32u[:, :],
                                          zer[:, 0:1].broadcast_to([32, 1024])), 1)
                v.wait_ge(sem["smO"].sem, ci + 1)
                if dbg & 4:
                    sem["sOe"].inc(
                        v.tensor_copy(outsb[:, cc * 32:cc * 32 + 1024],
                                      ps_out[:, :]), 1)
                else:
                    v.tensor_copy(
                        outsb[0:64, :].rearrange("p (q f) -> p q f", q=16)[
                            :, :, cc * 32:(cc + 1) * 32],
                        ps_out[0:64, :].rearrange("p (q f) -> p q f", q=16)[:, :, 0:32])
                    sem["sOe"].inc(
                        v.tensor_copy(
                            outsb[64:128, :].rearrange("p (q f) -> p q f", q=16)[
                                :, :, cc * 32:(cc + 1) * 32],
                            ps_out[64:128, :].rearrange("p (q f) -> p q f", q=16)[
                                :, :, 32:64]), 1)
                v.wait_ge(sem["smS"].sem, ci + 1)
                if dbg & 8:
                    chain(v, sem["svs"],
                          v.tensor_scalar_mul(Ssb[:, slot * 2048:(slot + 1) * 2048],
                                              Ssb[:, (1 - slot) * 2048:(2 - slot) * 2048],
                                              0.5))
                    sem["sSr"].inc(
                        v.tensor_add(Ssb[:, slot * 2048:(slot + 1) * 2048],
                                     Ssb[:, slot * 2048:(slot + 1) * 2048], ps_S[:, :]), 1)
                else:
                    chain(v, sem["svs"],
                          v.tensor_mul(Ssb[:, slot * 2048:(slot + 1) * 2048]
                                       .rearrange("p (q f) -> p q f", q=16),
                                       Ssb[:, (1 - slot) * 2048:(2 - slot) * 2048]
                                       .rearrange("p (q f) -> p q f", q=16),
                                       d32s[:, ha * 16:(ha + 1) * 16].unsqueeze(2)
                                       .broadcast_to([128, 16, 128])))
                    sem["sSr"].inc(
                        v.tensor_add(Ssb[:, slot * 2048:(slot + 1) * 2048],
                                     Ssb[:, slot * 2048:(slot + 1) * 2048], ps_S[:, :]), 1)
        blk.vector(v_chunks)

        def t_chunks(t, ha=ha, SHLn=SHLn, shl=shl, CI0=CI0):
            t.wait_ge(shl.sem, SHLn)
            t.wait_ge(sem["gld2"].sem, GLD2)
            if do_s0:
                t.wait_ge(sem["ss0e"].sem, ha + 1)
            if do_dkb:
                t.wait_ge(sem["sdkb"].sem, ha + 1)
            for cc in range(nch):
                ci = CI0 + cc
                tb = cc // 4
                band = 32 * (cc % 4)
                slot = cc % 2
                gslot = tb % 2
                t.wait_ge(sem["svp"].sem, ci + 1)
                t.wait_ge(sem["svgp"].sem, ha * 8 + tb + 1)
                t.wait_ge(sem["sSr"].sem, ci)
                t.wait_ge(sem["sOe"].sem, ci)
                t.wait_ge(sem["sAe"].sem, ci)
                if dbg & 64:
                    inst = t.matmul(ps_A[band:band + 32, 0:32],
                                    kh_c[0:64, slot * 512:slot * 512 + 32],
                                    rh_c[0:64, slot * 512:slot * 512 + 32],
                                    start=True, stop=True,
                                    tile_position=(0, band))
                    sem["smA"].inc(inst, 1)
                else:
                    for h in range(32):
                        q, hh = h // 2, h % 2
                        inst = t.matmul(
                            ps_A[band:band + 32, hh * 512 + q * 32:hh * 512 + (q + 1) * 32],
                            kh_c[hh * 64:hh * 64 + 64,
                                 slot * 512 + q * 32:slot * 512 + (q + 1) * 32],
                            rh_c[hh * 64:hh * 64 + 64,
                                 slot * 512 + q * 32:slot * 512 + (q + 1) * 32],
                            start=(h <= 1), stop=(h >= 30),
                            tile_position=(hh * 64, band))
                        if h == 31:
                            sem["smA"].inc(inst, 1)
                if dbg & 128:
                    inst = t.matmul(ps_S[:, 0:128],
                                    kbar[band:band + 32, 0:128],
                                    vtm[band:band + 32, tb * 2048:tb * 2048 + 128],
                                    start=True, stop=True,
                                    tile_position=(band, 0))
                    sem["smS"].inc(inst, 1)
                else:
                    for pr in range(16):
                        inst = t.matmul(
                            ps_S[:, pr * 128:(pr + 1) * 128],
                            kbar[band:band + 32, pr * 128:(pr + 1) * 128],
                            vtm[band:band + 32,
                                tb * 2048 + pr * 128:tb * 2048 + (pr + 1) * 128],
                            start=(pr % 4 == 0), stop=(pr % 4 == 3),
                            tile_position=(band, 0))
                        if pr == 15:
                            sem["smS"].inc(inst, 1)
                t.wait_ge(sem["sAe"].sem, ci + 1)
                if dbg & 32:
                    inst = t.matmul(ps_out[:, 0:64],
                                    vtm[band:band + 32, tb * 2048:tb * 2048 + 128],
                                    Asb[band:band + 32, 0:64],
                                    start=True, stop=True,
                                    tile_position=(band, 0))
                    sem["smO"].inc(inst, 1)
                else:
                    for pr in range(16):
                        t.matmul(ps_out[:, pr * 64:(pr + 1) * 64],
                                 vtm[band:band + 32,
                                     tb * 2048 + pr * 128:tb * 2048 + (pr + 1) * 128],
                                 Asb[band:band + 32, pr * 64:(pr + 1) * 64],
                                 start=(pr == 0 or pr == 8), stop=False,
                                 tile_position=(band, 0))
                    for h in range(32):
                        q, hh = h // 2, h % 2
                        t.matmul(ps_out[hh * 64:hh * 64 + 64,
                                        q * 64 + hh * 32:q * 64 + hh * 32 + 32],
                                 Ssb[hh * 64:hh * 64 + 64,
                                     (1 - slot) * 2048 + q * 128 + hh * 64:
                                     (1 - slot) * 2048 + q * 128 + hh * 64 + 64],
                                 rt_c[hh * 64:hh * 64 + 64,
                                      slot * 512 + q * 32:slot * 512 + (q + 1) * 32],
                                 start=False, stop=False,
                                 tile_position=(hh * 64, hh * 64))
                    for pr in range(16):
                        inst = t.matmul(
                            ps_out[:, pr * 64:(pr + 1) * 64],
                            vpb[band:band + 32, pr * 128:(pr + 1) * 128],
                            I32p[band:band + 32, :],
                            start=False, stop=(pr == 7 or pr == 15),
                            tile_position=(band, 0))
                        if pr == 15:
                            sem["smO"].inc(inst, 1)
        blk.tensor(t_chunks)

        def sy_fin(sy, ha=ha, CI0=CI0):
            sy.wait_ge(sem["sOe"].sem, sem["sOe"].n)
            for q in range(16):
                sem["sso"].inc(
                    sy.dma_start(
                        out=dram["scan_out"][(ha * 16 + q) * 128:
                                             (ha * 16 + q + 1) * 128, :],
                        in_=outsb[:, q * 1024:(q + 1) * 1024]), 16)
            sy.wait_ge(sem["sSr"].sem, sem["sSr"].n)
            sem["sso"].inc(
                sy.dma_start(out=dram["s_out"][:, ha * 2048:(ha + 1) * 2048],
                             in_=Ssb[:, 2048:4096]), 16)
        blk.sync(sy_fin)

    def sy_drain(sy):
        sy.wait_ge(sem["sso"].sem, sem["sso"].n)
    blk.sync(sy_drain)


def build_phase_b(nch=32, **kw):
    nc = bass.Bass("TRN2", target_bir_lowering=False)
    dram = {}
    for n, shp, dt in [("k_fm", [D, T], BF16), ("r_fm", [D, T], BF16),
                       ("k_tm", [T, D], BF16), ("v_tm", [T, D], BF16),
                       ("td_fm", [128, 32], F32), ("tf_fm", [128, 32], F32),
                       ("td_s", [128, 32], F32), ("td_row", [1, 4096], F32),
                       ("s_in", [128, 4096], BF16)]:
        dram[n] = nc.dram_tensor(n, shp, dt, kind="ExternalInput").ap()
    dram["scan_out"] = nc.dram_tensor("scan_out", [D, T], BF16, kind="ExternalOutput").ap()
    dram["s_out"] = nc.dram_tensor("s_out", [128, 4096], BF16, kind="ExternalOutput").ap()
    es = ExitStack()
    with es:
        sem = {n: Ctr(es.enter_context(nc.semaphore(n))) for n in SEMS_B}
        blk = es.enter_context(nc.Block())
        emit_phase_b(nc, es, blk, sem, dram, nch=nch, **kw)
    return nc


def host_inputs_phase_b(core, inp, afiles):
    td = np.asarray(inp["L0_attn_time_decay"]).astype(np.float32)
    tf = np.asarray(inp["L0_attn_time_faaaa"]).astype(np.float32)
    tdf = td.reshape(-1)
    d = {
        "td_fm": np.ascontiguousarray(tdf.reshape(32, 128).T),
        "tf_fm": np.ascontiguousarray(tf.reshape(-1).reshape(32, 128).T),
        "td_row": tdf.reshape(1, 4096),
    }
    td_s = np.zeros((128, 32), np.float32)
    for ha in range(2):
        for q in range(16):
            col = ha * 16 + q
            td_s[0:64, col] = td[32 * ha + 2 * q]
            td_s[64:128, col] = td[32 * ha + 2 * q + 1]
    d["td_s"] = td_s
    d.update(afiles)
    return d


SEMS_C = ["gms", "gld", "sgs", "svs", "ssb", "sld0", "sld1", "sscbm", "sscbv",
          "ssq", "sgnm", "svst", "sast", "sab", "sgt", "sgtd",
          "swl0", "swl1", "swl2", "swl3", "sfw0", "sfw1", "sfw2", "sfw3",
          "ssf", "smm", "sxl0", "sxl1", "sev", "sso0", "sso1"]
NSLAB = 4
INV64 = 1.0 / 64.0


def emit_phase_c(nc, es, blk, sem, dram):
    HW = 32 * 512
    osb = es.enter_context(nc.sbuf_tensor("osbC", [128, HW], BF16))
    gsb = es.enter_context(nc.sbuf_tensor("gsbC", [128, HW], BF16))
    xnsb = es.enter_context(nc.sbuf_tensor("xnsbC", [128, HW], BF16))
    gated = es.enter_context(nc.sbuf_tensor("gatedC", [128, HW], BF16))
    sqb = es.enter_context(nc.sbuf_tensor("sqbC", [128, 2 * 512], BF16))
    E64 = es.enter_context(nc.sbuf_tensor("E64C", [128, 32 * 64], BF16))
    sel64 = es.enter_context(nc.sbuf_tensor("sel64C", [64, 32 * 128], BF16))
    stats = es.enter_context(nc.sbuf_tensor("statsC", [64, 4 * 512], F32))
    stats_bf = es.enter_context(nc.sbuf_tensor("statsbfC", [64, 2 * 512], BF16))
    gnw = es.enter_context(nc.sbuf_tensor("gnwC", [128, 32], F32))
    gnb = es.enter_context(nc.sbuf_tensor("gnbC", [128, 32], F32))
    consts = es.enter_context(nc.sbuf_tensor("constsC", [128, 2], F32))
    scrow = es.enter_context(nc.sbuf_tensor("scrowC", [1, 4096], F32))
    scb = es.enter_context(nc.sbuf_tensor("scbC", [128, 4096], BF16))
    onesr = es.enter_context(nc.sbuf_tensor("onesrC", [1, 128], F32))
    slab = es.enter_context(nc.sbuf_tensor("slabC", [128, NSLAB * 512], BF16))
    fslab = es.enter_context(nc.sbuf_tensor("fslabC", [128, NSLAB * 512], BF16))
    xbuf = es.enter_context(nc.sbuf_tensor("xbufC", [128, 2 * 512], F32))
    x1sb = es.enter_context(nc.sbuf_tensor("x1sbC", [128, 2 * 512], F32))
    ps = es.enter_context(nc.psum_tensor("psC", [128, 4096], F32))

    def g_init(g):
        sem["gms"].inc(g.memset(consts[:, 0:1], 1e-5), 1)
        sem["gms"].inc(g.memset(onesr[:, :], 1.0), 1)
        chain(g, sem["sgs"], g.memset(E64[:, :], 0.0))
        for pt in range(32):
            g.memset(E64[0:64, pt * 64 + 2 * pt:pt * 64 + 2 * pt + 1], 1.0)
            inst = g.memset(E64[64:128, pt * 64 + 2 * pt + 1:pt * 64 + 2 * pt + 2], 1.0)
        sem["gms"].inc(inst, 1)
        sem["gld"].inc(g.dma_start(out=gnw[:, :], in_=dram["gnw"][:, :]), 16)
        sem["gld"].inc(g.dma_start(out=gnb[:, :], in_=dram["gnb"][:, :]), 16)
        sem["gld"].inc(g.dma_start(out=scrow[:, :], in_=dram["scales_row"][:, :]), 16)
        chain(g, sem["sgs"], g.memset(sel64[:, :], 0.0))
        for pt in range(32):
            sem["ssb"].inc(g.dma_start(
                out=sel64[2 * pt:2 * pt + 1, pt * 128:pt * 128 + 64],
                in_=dram["gnw"][0:64, pt:pt + 1].rearrange("p o -> o p"),
                allow_slow_non_contiguous=True), 16)
            sem["ssb"].inc(g.dma_start(
                out=sel64[2 * pt + 1:2 * pt + 2, pt * 128 + 64:pt * 128 + 128],
                in_=dram["gnw"][64:128, pt:pt + 1].rearrange("p o -> o p"),
                allow_slow_non_contiguous=True), 16)
        th = 0
        tsl = slice(0, 512)
        sld = sem["sld0"]
        for pt in range(32):
            sld.inc(g.dma_start(out=osb[:, pt * 512:(pt + 1) * 512],
                                in_=dram["scan_out"][pt * 128:(pt + 1) * 128, tsl]), 16)
            sld.inc(g.dma_start(out=gsb[:, pt * 512:(pt + 1) * 512],
                                in_=dram["g_fm"][pt * 128:(pt + 1) * 128, tsl]), 16)
            sld.inc(g.dma_start(out=xnsb[:, pt * 512:(pt + 1) * 512],
                                in_=dram["xn_fm"][pt * 128:(pt + 1) * 128, tsl]), 16)
    blk.gpsimd(g_init)

    def g_load1(g):
        tsl = slice(512, 1024)
        sld = sem["sld1"]
        g.wait_ge(sem["sgtd"].sem, 1)
        g.wait_ge(sem["ssf"].sem, 256)
        for pt in range(32):
            sld.inc(g.dma_start(out=osb[:, pt * 512:(pt + 1) * 512],
                                in_=dram["scan_out"][pt * 128:(pt + 1) * 128, tsl]), 16)
            sld.inc(g.dma_start(out=gsb[:, pt * 512:(pt + 1) * 512],
                                in_=dram["g_fm"][pt * 128:(pt + 1) * 128, tsl]), 16)
            sld.inc(g.dma_start(out=xnsb[:, pt * 512:(pt + 1) * 512],
                                in_=dram["xn_fm"][pt * 128:(pt + 1) * 128, tsl]), 16)
    GLOAD1 = g_load1
    GMS, GLD, SSB = sem["gms"].n, sem["gld"].n, sem["ssb"].n
    SLD = (96 * 16, 96 * 16)

    def t_scb(t):
        t.wait_ge(sem["gld"].sem, GLD)
        t.wait_ge(sem["gms"].sem, GMS)
        for nb in range(8):
            if nb > 0:
                t.wait_ge(sem["sscbv"].sem, nb)
            sem["sscbm"].inc(
                t.matmul(ps[:, 0:512], onesr[0:1, :],
                         scrow[0:1, nb * 512:(nb + 1) * 512],
                         start=True, stop=True), 1)
    blk.tensor(t_scb)

    def v_scb(v):
        for nb in range(8):
            v.wait_ge(sem["sscbm"].sem, nb + 1)
            sem["sscbv"].inc(
                v.tensor_copy(scb[:, nb * 512:(nb + 1) * 512], ps[:, 0:512]), 1)
    blk.vector(v_scb)

    def g_slabs_all(g, si_lo=0, si_hi=512):
        for th2 in range(2):
            for og in range(8):
                for i in range(32):
                    si = th2 * 256 + og * 32 + i
                    if not (si_lo <= si < si_hi):
                        continue
                    slot = si % NSLAB
                    if si >= NSLAB:
                        g.wait_ge(sem["ssf"].sem, si - NSLAB + 1)
                    sem[f"swl{slot}"].inc(
                        g.dma_start(out=slab[:, slot * 512:(slot + 1) * 512],
                                    in_=dram["woT"][i * 128:(i + 1) * 128,
                                                    og * 512:(og + 1) * 512]), 16)
                    sem[f"sfw{slot}"].inc(
                        g.dma_start(out=fslab[:, slot * 512:(slot + 1) * 512],
                                    in_=dram["wfT"][i * 128:(i + 1) * 128,
                                                    og * 512:(og + 1) * 512]), 16)
    blk.gpsimd(lambda g: g_slabs_all(g, 0, 256 + NSLAB))
    blk.gpsimd(GLOAD1)
    blk.gpsimd(lambda g: g_slabs_all(g, 256 + NSLAB, 512))

    for th in range(2):
        sld = sem[f"sld{th}"]
        SLDn = SLD[th]

        def a_sq(a, th=th, sld=sld, SLDn=SLDn):
            a.wait_ge(sld.sem, SLDn)
            for pt in range(32):
                s = pt % 2
                if pt >= 2:
                    a.wait_ge(sem["sgnm"].sem, th * 64 + 2 * (pt - 1))
                sem["ssq"].inc(
                    a.activation(sqb[:, s * 512:(s + 1) * 512],
                                 osb[:, pt * 512:(pt + 1) * 512], AF.Square), 1)
        blk.scalar(a_sq)

        def t_gn(t, th=th, sld=sld, SLDn=SLDn):
            t.wait_ge(sld.sem, SLDn)
            if th == 0:
                t.wait_ge(sem["sscbv"].sem, 8)
            else:
                t.wait_ge(sem["sev"].sem, 32)
            for pt in range(32):
                s = pt % 2
                t.wait_ge(sem["ssq"].sem, th * 32 + pt + 1)
                t.matmul(ps[0:64, 0:512], E64[:, pt * 64:(pt + 1) * 64],
                         osb[:, pt * 512:(pt + 1) * 512],
                         start=(pt == 0), stop=(pt == 31))
                inst = t.matmul(ps[0:64, 512:1024], E64[:, pt * 64:(pt + 1) * 64],
                                sqb[:, s * 512:(s + 1) * 512],
                                start=(pt == 0), stop=(pt == 31))
                sem["sgnm"].inc(inst, 2)
        blk.tensor(t_gn)

        def v_st(v, th=th):
            v.wait_ge(sem["sgnm"].sem, (th + 1) * 64)
            chain(v, sem["svs"],
                  v.tensor_scalar_mul(stats[:, 0:512], ps[0:64, 0:512], INV64))
            chain(v, sem["svs"],
                  v.tensor_mul(stats[:, 512:1024], stats[:, 0:512], stats[:, 0:512]))
            sem["svst"].inc(
                v.scalar_tensor_tensor(stats[:, 1024:1536], ps[0:64, 512:1024],
                                       INV64, stats[:, 512:1024],
                                       ALU.mult, ALU.subtract), 1)
            v.wait_ge(sem["sast"].sem, th + 1)
            chain(v, sem["svs"],
                  v.reciprocal(stats[:, 0:512] if False else stats[:, 1024:1536],
                               stats[:, 1536:2048]))
            chain(v, sem["svs"],
                  v.tensor_copy(stats_bf[:, 0:512], stats[:, 1024:1536]))
            sem["svst"].inc(
                v.tensor_mul(stats_bf[:, 512:1024], stats[:, 0:512],
                             stats[:, 1024:1536]), 1)
        blk.vector(v_st)

        def a_st(a, th=th):
            a.wait_ge(sem["svst"].sem, 2 * th + 1)
            sem["sast"].inc(
                a.activation(stats[:, 1536:2048], stats[:, 1024:1536], AF.Sqrt,
                             bias=consts[0:64, 0:1]), 1)
        blk.scalar(a_st)

        def t_sel(t, th=th):
            t.wait_ge(sem["svst"].sem, 2 * th + 2)
            t.wait_ge(sem["ssb"].sem, SSB)
            for pt in range(32):
                if pt > 0:
                    t.wait_ge(sem["sgt"].sem, th * 32 + pt)
                t.matmul(ps[:, 1024:1536], sel64[:, pt * 128:(pt + 1) * 128],
                         stats_bf[:, 0:512], start=True, stop=True)
                sem["sab"].inc(
                    t.matmul(ps[:, 1536:2048], sel64[:, pt * 128:(pt + 1) * 128],
                             stats_bf[:, 512:1024], start=True, stop=True), 1)
        blk.tensor(t_sel)

        def v_gt(v, th=th):
            for pt in range(32):
                v.wait_ge(sem["sab"].sem, th * 32 + pt + 1)
                chain(v, sem["svs"],
                      v.tensor_mul(gated[:, pt * 512:(pt + 1) * 512],
                                   osb[:, pt * 512:(pt + 1) * 512], ps[:, 1024:1536]))
                chain(v, sem["svs"],
                      v.scalar_tensor_tensor(gated[:, pt * 512:(pt + 1) * 512],
                                             ps[:, 1536:2048], -1.0,
                                             gated[:, pt * 512:(pt + 1) * 512],
                                             ALU.mult, ALU.add))
                chain(v, sem["svs"],
                      v.tensor_scalar_add(gated[:, pt * 512:(pt + 1) * 512],
                                          gated[:, pt * 512:(pt + 1) * 512],
                                          gnb[:, pt:pt + 1]))
                sem["sgt"].inc(
                    v.tensor_mul(gated[:, pt * 512:(pt + 1) * 512],
                                 gated[:, pt * 512:(pt + 1) * 512],
                                 gsb[:, pt * 512:(pt + 1) * 512]), 1)
            if th == 0:
                sem["sgtd"].inc(v.engine_nop(), 1)
        blk.vector(v_gt)




        def t_og(t, th=th):
            t.wait_ge(sem["sgt"].sem, th * 32 + 32)
            for og in range(8):
                gc = th * 8 + og
                if gc > 0:
                    t.wait_ge(sem["sev"].sem, 4 * gc)
                for i in range(32):
                    si = th * 256 + og * 32 + i
                    slot = si % NSLAB
                    t.wait_ge(sem[f"swl{slot}"].sem, 16 * (si // NSLAB + 1))
                    t.wait_ge(sem[f"sfw{slot}"].sem, 16 * (si // NSLAB + 1))
                    for tt in range(4):
                        inst_a = t.matmul(
                            ps[:, tt * 512:(tt + 1) * 512],
                            gated[:, i * 512 + tt * 128:i * 512 + (tt + 1) * 128],
                            slab[:, slot * 512:(slot + 1) * 512],
                            start=(i == 0), stop=(i == 31))
                        inst_f = t.matmul(
                            ps[:, 2048 + tt * 512:2048 + (tt + 1) * 512],
                            xnsb[:, i * 512 + tt * 128:i * 512 + (tt + 1) * 128],
                            fslab[:, slot * 512:(slot + 1) * 512],
                            start=(i == 0), stop=(i == 31))
                        if tt == 3:
                            sem["ssf"].inc(inst_f, 1)
        blk.tensor(t_og)

        def sy_ep(sy, th=th):
            for og in range(8):
                gc = th * 8 + og
                for tt in range(4):
                    idx = gc * 4 + tt
                    s = idx % 2
                    if idx >= 2:
                        sy.wait_ge(sem["sev"].sem, idx - 1)
                    sem[f"sxl{s}"].inc(
                        sy.dma_start(out=xbuf[:, s * 512:(s + 1) * 512],
                                     in_=dram["x_sl"][th * 512 + tt * 128:
                                                      th * 512 + (tt + 1) * 128,
                                                      og * 512:(og + 1) * 512]), 16)
                    sy.wait_ge(sem["sev"].sem, idx + 1)
                    sem[f"sso{s}"].inc(
                        sy.dma_start(out=dram["x1"][th * 512 + tt * 128:
                                                    th * 512 + (tt + 1) * 128,
                                                    og * 512:(og + 1) * 512],
                                     in_=x1sb[:, s * 512:(s + 1) * 512]), 16)
            if th == 1:
                for pp in range(2):
                    sy.wait_ge(sem[f"sso{pp}"].sem, sem[f"sso{pp}"].n)
        blk.sync(sy_ep)

        def v_ep(v, th=th):
            for og in range(8):
                gc = th * 8 + og
                v.wait_ge(sem["ssf"].sem, th * 256 + (og + 1) * 32)
                for tt in range(4):
                    idx = gc * 4 + tt
                    s = idx % 2
                    v.wait_ge(sem[f"sxl{s}"].sem, 16 * (idx // 2 + 1))
                    if idx >= 2:
                        v.wait_ge(sem[f"sso{s}"].sem, 16 * (idx // 2))
                    chain(v, sem["svs"],
                          v.tensor_add(x1sb[:, s * 512:(s + 1) * 512],
                                       xbuf[:, s * 512:(s + 1) * 512],
                                       ps[:, tt * 512:(tt + 1) * 512]))
                    chain(v, sem["svs"],
                          v.tensor_mul(xbuf[:, s * 512:(s + 1) * 512],
                                       scb[:, og * 512:(og + 1) * 512],
                                       ps[:, 2048 + tt * 512:2048 + (tt + 1) * 512]))
                    sem["sev"].inc(
                        v.tensor_add(x1sb[:, s * 512:(s + 1) * 512],
                                     x1sb[:, s * 512:(s + 1) * 512],
                                     xbuf[:, s * 512:(s + 1) * 512]), 1)
        blk.vector(v_ep)


def build_phase_c():
    nc = bass.Bass("TRN2", target_bir_lowering=False)
    dram = {}
    for n, shp, dt in [("scan_out", [D, T], BF16), ("g_fm", [D, T], BF16),
                       ("xn_fm", [D, T], BF16), ("x_sl", [T, D], F32),
                       ("woT", [D, D], F32), ("wfT", [D, D], I8),
                       ("scales_row", [1, D], F16),
                       ("gnw", [128, 32], F32), ("gnb", [128, 32], F32)]:
        dram[n] = nc.dram_tensor(n, shp, dt, kind="ExternalInput").ap()
    dram["x1"] = nc.dram_tensor("x1", [T, D], F32, kind="ExternalOutput").ap()
    es = ExitStack()
    with es:
        sem = {n: Ctr(es.enter_context(nc.semaphore(n))) for n in SEMS_C}
        blk = es.enter_context(nc.Block())
        emit_phase_c(nc, es, blk, sem, dram)
    return nc


def host_inputs_phase_c(core, inp, files):
    d = {
        "woT": np.ascontiguousarray(np.asarray(inp["L0_attn_w_output"]).T),
        "wfT": np.ascontiguousarray(np.asarray(inp["L0_ffn_weight"]).T),
        "scales_row": np.asarray(inp["L0_ffn_scales"]).reshape(1, D),
        "gnw": np.ascontiguousarray(
            np.asarray(inp["L0_attn_gn_weight"]).reshape(32, 128).T),
        "gnb": np.ascontiguousarray(
            np.asarray(inp["L0_attn_gn_bias"]).reshape(32, 128).T),
    }
    d.update(files)
    return d


STR = 1056
INV_D = 1.0 / 4096.0
NSLAB = 4
SEMS_D = ["gms", "gld", "gld2", "sgs", "sv2", "svs", "sxl0", "sxl1", "sacast", "sasq",
          "sasq2", "stp0", "stp1", "saht", "svh", "smhb",
          "swl0", "swl1", "swl2", "swl3", "ssf", "sev",
          "smst", "sast", "svz", "smsum", "svrc", "smbc", "svpn", "smot",
          "sevo", "smsc", "svtk", "sagx", "svgt", "sgt", "smrep",
          "sxf0", "sxf1", "sefin", "sso0", "sso1"]

SI_CVQ, SI_QH = 0, 192
SI_CVK, SI_KH = 208, 400
SI_CVV, SI_VH = 416, 608
SI_WO, SI_WQ, SI_A, SI_FIN = 624, 640, 672, 704
SI_END = 896


def emit_phase_d(nc, es, blk, sem, dram, dbg=False):
    xn1 = es.enter_context(nc.sbuf_tensor("xn1D", [128, 32 * STR], BF16))
    slab = es.enter_context(nc.sbuf_tensor("slabD", [128, NSLAB * 512], BF16))
    consts = es.enter_context(nc.sbuf_tensor("constsD", [128, 1], F32))
    onesr = es.enter_context(nc.sbuf_tensor("onesrD", [65, 128], F32))
    onesrb = es.enter_context(nc.sbuf_tensor("onesrbD", [1, 128], BF16))
    onesc = es.enter_context(nc.sbuf_tensor("onescD", [128, 1], BF16))
    m12 = es.enter_context(nc.sbuf_tensor("m12D", [128, 1], F32))
    zer = es.enter_context(nc.sbuf_tensor("zerD", [128, 1], BF16))
    m2048 = es.enter_context(nc.sbuf_tensor("m2048D", [128, 2048], I8))
    convb = es.enter_context(nc.sbuf_tensor("convbD", [128, 24], F32))
    bqc = es.enter_context(nc.sbuf_tensor("bqcD", [128, 8], F32))
    bkc = es.enter_context(nc.sbuf_tensor("bkcD", [128, 8], F32))
    bvr = es.enter_context(nc.sbuf_tensor("bvrD", [1, 1024], BF16))
    boc = es.enter_context(nc.sbuf_tensor("bocD", [128, 8], F32))
    pjr = es.enter_context(nc.sbuf_tensor("pjrD", [1, 4096], BF16))
    keysT = es.enter_context(nc.sbuf_tensor("keysTD", [16, 64], BF16))
    REPsb = es.enter_context(nc.sbuf_tensor("REPD", [64, 512], BF16))
    stsb = es.enter_context(nc.sbuf_tensor("stsbD", [128, 2048], BF16))
    reciprow = es.enter_context(nc.sbuf_tensor("reciprowD", [1, 2048], F32))
    qr = es.enter_context(nc.sbuf_tensor("qrD", [16, 1024], BF16))
    scf = es.enter_context(nc.sbuf_tensor("scfD", [128, 512], F32))
    scw = es.enter_context(nc.sbuf_tensor("scwD", [128, 512], F32))
    maskf = es.enter_context(nc.sbuf_tensor("maskfD", [128, 512], F32))
    m16 = es.enter_context(nc.sbuf_tensor("m16D", [128, 128], F32))
    s8 = es.enter_context(nc.sbuf_tensor("s8D", [128, 16], F32))
    gate_tm = es.enter_context(nc.sbuf_tensor("gatetmD", [128, 1024], BF16))
    gate_fm = es.enter_context(nc.sbuf_tensor("gatefmD", [128, 1024], BF16))
    xf = es.enter_context(nc.sbuf_tensor("xfD", [128, 2 * 512], F32))
    ysb = es.enter_context(nc.sbuf_tensor("ysbD", [128, 2 * 512], F32))
    ps = es.enter_context(nc.psum_tensor("psD", [128, 4096], F32))

    ROW0 = [tt * 128 for tt in range(8)]

    def g_init(g):
        sem["gms"].inc(g.memset(consts[:, :], 1e-5), 1)
        sem["gms"].inc(g.memset(onesr[:, :], 1.0), 1)
        sem["gms"].inc(g.memset(onesrb[:, :], 1.0), 1)
        sem["gms"].inc(g.memset(onesc[:, :], 1.0), 1)
        sem["gms"].inc(g.memset(m12[:, :], -12.0), 1)
        sem["gms"].inc(g.memset(zer[:, :], 0.0), 1)
        sem["gms"].inc(g.memset(gate_tm[:, :], 0.0), 1)
        chain(g, sem["sgs"], g.memset(m2048[:, :], 1))
        sem["gms"].inc(g.affine_select(m2048[:, :], m2048[:, :],
                                       [[0, 16], [-1, 128]], ALU.is_ge, 0,
                                       base=-1, channel_multiplier=1), 1)
        for n, b in [("convb", convb), ("bq", bqc), ("bk", bkc),
                     ("bv_row", bvr), ("bo", boc), ("projb_row", pjr),
                     ("keysT", keysT), ("REP", REPsb)]:
            sem["gld"].inc(g.dma_start(out=b[:, :], in_=dram[n][:, :]), 16)
    blk.gpsimd(g_init)
    GMS, GLD = sem["gms"].n, sem["gld"].n

    def v_binit(v):
        v.wait_ge(sem["gld"].sem, GLD)
        v.wait_ge(sem["gms"].sem, GMS)
        sem["sv2"].inc(v.tensor_scalar_mul(bqc[:, :], bqc[:, :], 0.125), 1)
    blk.vector(v_binit)

    with (
        nc.sbuf_tensor("xbufD", [128, 2 * D], F32) as xbuf,
        nc.sbuf_tensor("sqbufD", [128, D], BF16) as sqbuf,
        nc.sbuf_tensor("lnstD", [128, 16], F32) as stat,
        nc.sbuf_tensor("lntmD", [128, 2 * D], BF16) as xnbuf,
        nc.sbuf_tensor("hrowD", [65, D], F32) as hrow,
        nc.sbuf_tensor("hcolD", [128, 64], F32) as hcol,
        nc.sbuf_tensor("hc16D", [128, 64], BF16) as hc16,
        nc.sbuf_tensor("pstatD", [65, 8], F32) as pstat,
    ):
        def g_init2(g):
            sem["gld2"].inc(g.dma_start(out=hrow[0:1, :], in_=dram["xh_rows"][0:1, :]), 16)
            sem["gld2"].inc(g.dma_start(out=hrow[64:65, :], in_=dram["xh_rows"][1:2, :]), 16)
            sem["gld2"].inc(g.dma_start(out=hcol[:, :], in_=dram["xh_cols"][:, :]), 16)
        blk.gpsimd(g_init2)
        GLD2 = sem["gld2"].n

        def sy_ln(sy):
            for tt in range(8):
                s = tt % 2
                if tt >= 2:
                    sy.wait_ge(sem["sacast"].sem, tt - 1)
                sem[f"sxl{s}"].inc(
                    sy.dma_start(out=xbuf[:, s * D:(s + 1) * D],
                                 in_=dram["x1e"][1 + ROW0[tt]:1 + ROW0[tt] + 128, :]), 16)
                if tt >= 1:
                    pt = tt - 1
                    sp = pt % 2
                    sy.wait_ge(sem["sacast"].sem, pt + 1)
                    sem[f"stp{sp}"].inc(
                        sy.dma_start(
                            xn1[:, :].rearrange("p (db f) -> p db f", db=32)[
                                :, :, 16 + ROW0[pt]:16 + ROW0[pt] + 128],
                            xnbuf[:, sp * D:(sp + 1) * D].rearrange(
                                "p (db f) -> p db f", db=32),
                            transpose=True), 16)
            pt = 7
            sp = pt % 2
            sy.wait_ge(sem["sacast"].sem, pt + 1)
            sem[f"stp{sp}"].inc(
                sy.dma_start(
                    xn1[:, :].rearrange("p (db f) -> p db f", db=32)[
                        :, :, 16 + ROW0[pt]:16 + ROW0[pt] + 128],
                    xnbuf[:, sp * D:(sp + 1) * D].rearrange(
                        "p (db f) -> p db f", db=32),
                    transpose=True), 16)
        blk.sync(sy_ln)
        STP = (sem["stp0"].n, sem["stp1"].n)

        def v_ln(v):
            for tt in range(8):
                s = tt % 2
                c = 8 * s
                v.wait_ge(sem[f"sxl{s}"].sem, 16 * (tt // 2 + 1))
                chain(v, sem["svs"],
                      v.reduce_sum(stat[:, c + 0:c + 1],
                                   xbuf[:, s * D:(s + 1) * D], axis=AX.X))
                v.wait_ge(sem["sasq"].sem, tt + 1)
                chain(v, sem["svs"],
                      v.tensor_scalar_mul(stat[:, c + 2:c + 3],
                                          stat[:, c + 0:c + 1], INV_D))
                chain(v, sem["svs"],
                      v.tensor_mul(stat[:, c + 3:c + 4], stat[:, c + 2:c + 3],
                                   stat[:, c + 2:c + 3]))
                sem["sv2"].inc(
                    v.scalar_tensor_tensor(stat[:, c + 4:c + 5],
                                           stat[:, c + 1:c + 2], INV_D,
                                           stat[:, c + 3:c + 4],
                                           ALU.mult, ALU.subtract), 1)
                v.wait_ge(sem["sasq2"].sem, tt + 1)
                chain(v, sem["svs"],
                      v.reciprocal(stat[:, c + 6:c + 7], stat[:, c + 5:c + 6]))
                chain(v, sem["svs"],
                      v.scalar_tensor_tensor(stat[:, c + 7:c + 8],
                                             stat[:, c + 2:c + 3], -1.0,
                                             stat[:, c + 6:c + 7],
                                             ALU.mult, ALU.mult))
                if tt >= 2:
                    sp2 = (tt - 2) % 2
                    v.wait_ge(sem[f"stp{sp2}"].sem, 16 * ((tt - 2) // 2 + 1))
                sem["sacast"].inc(
                    v.tensor_scalar(xnbuf[:, s * D:(s + 1) * D],
                                    xbuf[:, s * D:(s + 1) * D],
                                    stat[:, c + 6:c + 7], stat[:, c + 7:c + 8],
                                    ALU.mult, ALU.add), 1)
        blk.vector(v_ln)

        def a_ln(a):
            a.wait_ge(sem["gms"].sem, GMS)
            for tt in range(8):
                s = tt % 2
                c = 8 * s
                a.wait_ge(sem[f"sxl{s}"].sem, 16 * (tt // 2 + 1))
                sem["sasq"].inc(
                    a.activation(sqbuf[:, :], xbuf[:, s * D:(s + 1) * D],
                                 AF.Square, accum_out=stat[:, c + 1:c + 2]), 1)
                a.wait_ge(sem["sv2"].sem, tt + 2)
                sem["sasq2"].inc(
                    a.activation(stat[:, c + 5:c + 6], stat[:, c + 4:c + 5],
                                 AF.Sqrt, bias=consts[:, 0:1]), 1)
        blk.scalar(a_ln)

        def v_lnh(v):
            v.wait_ge(sem["gld2"].sem, GLD2)
            for k, hi in enumerate((0, 64)):
                r = slice(hi, hi + 1)
                chain(v, sem["svs"],
                      v.reduce_sum(pstat[r, 0:1], hrow[r, :], axis=AX.X))
                v.wait_ge(sem["saht"].sem, 2 * k + 1)
                chain(v, sem["svs"],
                      v.tensor_scalar_mul(pstat[r, 2:3], pstat[r, 0:1], INV_D))
                chain(v, sem["svs"],
                      v.tensor_mul(pstat[r, 3:4], pstat[r, 2:3], pstat[r, 2:3]))
                sem["sv2"].inc(
                    v.scalar_tensor_tensor(pstat[r, 4:5], pstat[r, 1:2], INV_D,
                                           pstat[r, 3:4], ALU.mult,
                                           ALU.subtract), 1)
                v.wait_ge(sem["saht"].sem, 2 * k + 2)
                chain(v, sem["svs"], v.reciprocal(pstat[r, 6:7], pstat[r, 5:6]))
                sem["svh"].inc(
                    v.scalar_tensor_tensor(pstat[r, 7:8], pstat[r, 2:3], -1.0,
                                           pstat[r, 6:7], ALU.mult, ALU.mult), 1)
        blk.vector(v_lnh)

        def a_lnh(a):
            a.wait_ge(sem["sasq"].sem, 8)
            a.wait_ge(sem["gld2"].sem, GLD2)
            for k, hi in enumerate((0, 64)):
                r = slice(hi, hi + 1)
                sem["saht"].inc(
                    a.activation(sqbuf[r, :], hrow[r, :], AF.Square,
                                 accum_out=pstat[r, 1:2]), 1)
                a.wait_ge(sem["sv2"].sem, 10 + k)
                sem["saht"].inc(
                    a.activation(pstat[r, 5:6], pstat[r, 4:5], AF.Sqrt,
                                 bias=consts[r, 0:1]), 1)
        blk.scalar(a_lnh)

        def t_lnh(t):
            t.wait_ge(sem["svh"].sem, 2)
            sem["smhb"].inc(
                t.matmul(ps[:, 0:2], onesr[0:1, :], pstat[0:1, 6:8],
                         start=True, stop=True), 1)
            sem["smhb"].inc(
                t.matmul(ps[:, 512:514], onesr[64:65, :], pstat[64:65, 6:8],
                         start=True, stop=True, tile_position=(64, 0)), 1)
        blk.tensor(t_lnh)

        def v_lnh2(v):
            v.wait_ge(sem["smhb"].sem, 2)
            chain(v, sem["svs"],
                  v.scalar_tensor_tensor(hc16[:, 0:32], hcol[:, 0:32],
                                         ps[:, 0:1],
                                         ps[:, 1:2].broadcast_to([128, 32]),
                                         ALU.mult, ALU.add))
            chain(v, sem["svs"],
                  v.scalar_tensor_tensor(hc16[:, 32:64], hcol[:, 32:64],
                                         ps[:, 512:513],
                                         ps[:, 513:514].broadcast_to([128, 32]),
                                         ALU.mult, ALU.add))
            v.wait_ge(sem["stp0"].sem, STP[0])
            v.wait_ge(sem["stp1"].sem, STP[1])
            chain(v, sem["svs"],
                  v.tensor_copy(xn1[:, :].rearrange("p (db f) -> p db f", db=32)[
                      :, :, 15:16].squeeze(2), hc16[:, 0:32]))
            sem["svh"].inc(
                v.tensor_copy(xn1[:, :].rearrange("p (db f) -> p db f", db=32)[
                    :, :, 1040:1041].squeeze(2), hc16[:, 32:64]), 1)
        blk.vector(v_lnh2)
        SVH = sem["svh"].n

    convg = es.enter_context(nc.sbuf_tensor("convgD", [128, 8 * 1024], BF16))
    qh = es.enter_context(nc.sbuf_tensor("qhD", [128, 8 * 1024], BF16))
    kh = es.enter_context(nc.sbuf_tensor("khD", [128, 8 * 1024], BF16))
    vh = es.enter_context(nc.sbuf_tensor("vhD", [128, 8 * 1024], BF16))
    o1 = es.enter_context(nc.sbuf_tensor("o1D", [128, 8 * 1024], BF16))
    oT = convg
    zsb = kh

    def g_slabs(g):
        def ld(si, src):
            slot = si % NSLAB
            if si >= NSLAB:
                g.wait_ge(sem["ssf"].sem, si - NSLAB + 1)
            sem[f"swl{slot}"].inc(
                g.dma_start(out=slab[:, slot * 512:slot * 512 + src.shape[-1]],
                            in_=src), 16)
        si = 0
        for grp in range(3):
            for cg in range(2):
                cgg = grp * 2 + cg
                for i in range(32):
                    for ki in range(3):
                        ld(si, dram["convw"][ki, i * 128:(i + 1) * 128,
                                             cgg * 512:(cgg + 1) * 512])
                        si += 1
            wn = ["wq", "wk", "wv"][grp]
            for hec in range(2):
                for i in range(8):
                    ld(si, dram[wn][i * 128:(i + 1) * 128,
                                    hec * 512:(hec + 1) * 512])
                    si += 1
        for hec in range(2):
            for i in range(8):
                ld(si, dram["wo2"][i * 128:(i + 1) * 128,
                                   hec * 512:(hec + 1) * 512])
                si += 1
        for i in range(32):
            ld(si, dram["WqT"][i * 128:(i + 1) * 128, :])
            si += 1
        for i in range(32):
            ld(si, dram["AT"][i * 128:(i + 1) * 128, :])
            si += 1
        for th in range(2):
            for og in range(8):
                for i in range(8):
                    ld(si, dram["projw"][i * 128:(i + 1) * 128,
                                         og * 512:(og + 1) * 512])
                    si += 1
                for i in range(4):
                    ld(si, dram["BT"][i * 128:(i + 1) * 128,
                                      og * 512:(og + 1) * 512])
                    si += 1
        assert si == SI_END
    blk.gpsimd(g_slabs)

    def t_main(t):
        t.wait_ge(sem["svh"].sem, SVH)
        si = [0]
        fidx = [0]

        def slot_wait():
            s = si[0] % NSLAB
            t.wait_ge(sem[f"swl{s}"].sem, 16 * (si[0] // NSLAB + 1))
            return s

        def fill_start():
            fidx[0] += 1
            if fidx[0] > 1:
                t.wait_ge(sem["sev"].sem, fidx[0] - 1)

        for grp in range(3):
            for cg in range(2):
                fill_start()
                for i in range(32):
                    for ki in range(3):
                        s = slot_wait()
                        first = (i == 0 and ki == 0)
                        last = (i == 31 and ki == 2)
                        for ct in range(4):
                            for t2 in range(2):
                                inst = t.matmul(
                                    ps[:, (ct * 2 + t2) * 512:(ct * 2 + t2 + 1) * 512],
                                    slab[:, s * 512 + ct * 128:s * 512 + (ct + 1) * 128],
                                    xn1[:, i * STR + 15 + ki + t2 * 512:
                                        i * STR + 15 + ki + t2 * 512 + 512],
                                    start=first, stop=last)
                        sem["ssf"].inc(inst, 1)
                        si[0] += 1
            if grp < 2:
                for hec in range(2):
                    fill_start()
                    for i in range(8):
                        s = slot_wait()
                        for d4 in range(4):
                            for t2 in range(2):
                                inst = t.matmul(
                                    ps[:, (d4 * 2 + t2) * 512:(d4 * 2 + t2 + 1) * 512],
                                    slab[:, s * 512 + d4 * 128:s * 512 + (d4 + 1) * 128],
                                    convg[:, i * 1024 + t2 * 512:i * 1024 + (t2 + 1) * 512],
                                    start=(i == 0), stop=(i == 7))
                        sem["ssf"].inc(inst, 1)
                        si[0] += 1
            else:
                for hec in range(2):
                    fill_start()
                    for t8 in range(8):
                        t.matmul(ps[:, t8 * 512:(t8 + 1) * 512],
                                 onesrb[0:1, :],
                                 bvr[0:1, hec * 512:(hec + 1) * 512],
                                 start=True, stop=False)
                    for i in range(8):
                        s = slot_wait()
                        for t8 in range(8):
                            inst = t.matmul(
                                ps[:, t8 * 512:(t8 + 1) * 512],
                                convg[:, i * 1024 + t8 * 128:i * 1024 + (t8 + 1) * 128],
                                slab[:, s * 512:(s + 1) * 512],
                                start=False, stop=(i == 7))
                        sem["ssf"].inc(inst, 1)
                        si[0] += 1
        assert si[0] == SI_WO and fidx[0] == 12

        t.wait_ge(sem["sev"].sem, 12)
        for b in range(8):
            t.wait_ge(sem["svpn"].sem, b)
            for h in list(range(0, 16, 2)) + list(range(1, 16, 2)):
                hb = (h % 2) * 64
                inst = t.matmul(
                    ps[:, (h % 2) * 1024 + (h // 2) * 128:
                       (h % 2) * 1024 + (h // 2) * 128 + 128],
                    kh[hb:hb + 64, (h // 2) * 1024 + b * 128:
                       (h // 2) * 1024 + b * 128 + 128],
                    qh[hb:hb + 64, (h // 2) * 1024 + b * 128:
                       (h // 2) * 1024 + b * 128 + 128],
                    start=True, stop=True, tile_position=(hb, 0))
            sem["smst"].inc(inst, 1)
            t.wait_ge(sem["svz"].sem, b + 1)
            t.wait_ge(sem["svrc"].sem, b)
            t.wait_ge(sem["sevo"].sem, b)
            for h in range(16):
                inst = t.matmul(ps[0:1, 2048 + h * 128:2048 + (h + 1) * 128],
                                onesc[:, :], stsb[:, h * 128:(h + 1) * 128],
                                start=True, stop=True)
            sem["smsum"].inc(inst, 1)
            t.wait_ge(sem["svrc"].sem, b + 1)
            for h in range(16):
                inst = t.matmul(ps[:, h * 128:(h + 1) * 128],
                                onesr[0:1, :],
                                reciprow[0:1, h * 128:(h + 1) * 128],
                                start=True, stop=True)
            sem["smbc"].inc(inst, 1)
            t.wait_ge(sem["svpn"].sem, b + 1)
            for h in range(16):
                inst = t.matmul(
                    ps[(h % 2) * 64:(h % 2) * 64 + 64,
                       2048 + (h // 2) * 128:2048 + (h // 2) * 128 + 128],
                    vh[:, b * 1024 + h * 64:b * 1024 + h * 64 + 64],
                    stsb[:, h * 128:(h + 1) * 128],
                    start=True, stop=True, tile_position=(0, (h % 2) * 64))
            sem["smot"].inc(inst, 1)

        t.wait_ge(sem["sevo"].sem, 8)
        for hec in range(2):
            fill_start()
            for i in range(8):
                s = slot_wait()
                for d4 in range(4):
                    for t2 in range(2):
                        inst = t.matmul(
                            ps[:, (d4 * 2 + t2) * 512:(d4 * 2 + t2 + 1) * 512],
                            slab[:, s * 512 + d4 * 128:s * 512 + (d4 + 1) * 128],
                            oT[:, i * 1024 + t2 * 512:i * 1024 + (t2 + 1) * 512],
                            start=(i == 0), stop=(i == 7))
                sem["ssf"].inc(inst, 1)
                si[0] += 1
        fill_start()
        for i in range(32):
            s = slot_wait()
            for t2 in range(2):
                inst = t.matmul(
                    ps[0:16, t2 * 512:(t2 + 1) * 512],
                    slab[:, s * 512:s * 512 + 16],
                    xn1[:, i * STR + 16 + t2 * 512:i * STR + 16 + t2 * 512 + 512],
                    start=(i == 0), stop=(i == 31))
            sem["ssf"].inc(inst, 1)
            si[0] += 1
        fill_start()
        for t8 in range(8):
            inst = t.matmul(ps[:, t8 * 64:(t8 + 1) * 64],
                            qr[:, t8 * 128:(t8 + 1) * 128], keysT[:, :],
                            start=True, stop=True)
        sem["smsc"].inc(inst, 1)
        fill_start()
        for i in range(32):
            s = slot_wait()
            for n4 in range(4):
                for t2 in range(2):
                    inst = t.matmul(
                        ps[:, (n4 * 2 + t2) * 512:(n4 * 2 + t2 + 1) * 512],
                        slab[:, s * 512 + n4 * 128:s * 512 + (n4 + 1) * 128],
                        xn1[:, i * STR + 16 + t2 * 512:i * STR + 16 + t2 * 512 + 512],
                        start=(i == 0), stop=(i == 31))
            sem["ssf"].inc(inst, 1)
            si[0] += 1
        assert si[0] == SI_FIN
        fill_start()
        t.wait_ge(sem["sgt"].sem, 16)
        for n4 in range(4):
            for t2 in range(2):
                inst = t.matmul(ps[:, (n4 * 2 + t2) * 512:(n4 * 2 + t2 + 1) * 512],
                                REPsb[:, n4 * 128:(n4 + 1) * 128],
                                gate_fm[0:64, t2 * 512:(t2 + 1) * 512],
                                start=True, stop=True)
        sem["smrep"].inc(inst, 1)
        for th in range(2):
            for og in range(8):
                ff = th * 8 + og
                if ff == 0:
                    t.wait_ge(sem["sev"].sem, 18)
                else:
                    t.wait_ge(sem["sefin"].sem, 4 * ff)
                for t4 in range(4):
                    t.matmul(ps[:, t4 * 512:(t4 + 1) * 512],
                             onesrb[0:1, :],
                             pjr[0:1, og * 512:(og + 1) * 512],
                             start=True, stop=False)
                for i in range(8):
                    s = slot_wait()
                    for t4 in range(4):
                        inst = t.matmul(
                            ps[:, t4 * 512:(t4 + 1) * 512],
                            o1[:, i * 1024 + th * 512 + t4 * 128:
                               i * 1024 + th * 512 + (t4 + 1) * 128],
                            slab[:, s * 512:(s + 1) * 512],
                            start=False, stop=(i == 7))
                    sem["ssf"].inc(inst, 1)
                    si[0] += 1
                for i in range(4):
                    s = slot_wait()
                    for t4 in range(4):
                        inst = t.matmul(
                            ps[:, 2048 + t4 * 512:2048 + (t4 + 1) * 512],
                            zsb[:, i * 1024 + th * 512 + t4 * 128:
                                i * 1024 + th * 512 + (t4 + 1) * 128],
                            slab[:, s * 512:(s + 1) * 512],
                            start=(i == 0), stop=(i == 3))
                    sem["ssf"].inc(inst, 1)
                    si[0] += 1
        assert si[0] == SI_END
    blk.tensor(t_main)

    def v_main(v):
        for grp in range(3):
            for cg in range(2):
                cgg = grp * 2 + cg
                v.wait_ge(sem["ssf"].sem,
                          [SI_CVQ, SI_CVK, SI_CVV][grp] + 96 * (cg + 1))
                for ct in range(4):
                    co = cg * 4 + ct
                    for t2 in range(2):
                        inst = v.scalar_tensor_tensor(
                            convg[:, co * 1024 + t2 * 512:co * 1024 + (t2 + 1) * 512],
                            ps[:, (ct * 2 + t2) * 512:(ct * 2 + t2 + 1) * 512],
                            1.0,
                            convb[:, cgg * 4 + ct:cgg * 4 + ct + 1]
                            .broadcast_to([128, 512]),
                            ALU.mult, ALU.add)
                sem["sev"].inc(inst, 1)
            if grp < 2:
                dst, bc = (qh, bqc) if grp == 0 else (kh, bkc)
                sc = 0.125 if grp == 0 else 1.0
                for hec in range(2):
                    v.wait_ge(sem["ssf"].sem, [SI_QH, SI_KH][grp] + 8 * (hec + 1))
                    for d4 in range(4):
                        db = hec * 4 + d4
                        for t2 in range(2):
                            inst = v.scalar_tensor_tensor(
                                dst[:, db * 1024 + t2 * 512:db * 1024 + (t2 + 1) * 512],
                                ps[:, (d4 * 2 + t2) * 512:(d4 * 2 + t2 + 1) * 512],
                                sc, bc[:, db:db + 1].broadcast_to([128, 512]),
                                ALU.mult, ALU.add)
                    sem["sev"].inc(inst, 1)
            else:
                for hec in range(2):
                    v.wait_ge(sem["ssf"].sem, SI_VH + 8 * (hec + 1))
                    for t8 in range(8):
                        inst = v.tensor_copy(
                            vh[:, t8 * 1024 + hec * 512:t8 * 1024 + (hec + 1) * 512],
                            ps[:, t8 * 512:(t8 + 1) * 512])
                    sem["sev"].inc(inst, 1)
        for b in range(8):
            v.wait_ge(sem["sast"].sem, b + 1)
            sem["svz"].inc(
                v.copy_predicated(stsb[:, :], m2048[:, :],
                                  zer[:, 0:1].broadcast_to([128, 2048])), 1)
            v.wait_ge(sem["smsum"].sem, b + 1)
            sem["svrc"].inc(
                v.reciprocal(reciprow[0:1, :], ps[0:1, 2048:4096]), 1)
            v.wait_ge(sem["smbc"].sem, b + 1)
            sem["svpn"].inc(
                v.tensor_mul(stsb[:, :], stsb[:, :], ps[:, 0:2048]), 1)
            v.wait_ge(sem["smot"].sem, b + 1)
            for db in range(8):
                inst = v.tensor_copy(
                    oT[:, db * 1024 + b * 128:db * 1024 + (b + 1) * 128],
                    ps[:, 2048 + db * 128:2048 + (db + 1) * 128])
            sem["sevo"].inc(inst, 1)
        for hec in range(2):
            v.wait_ge(sem["ssf"].sem, SI_WO + 8 * (hec + 1))
            for d4 in range(4):
                db = hec * 4 + d4
                for t2 in range(2):
                    inst = v.scalar_tensor_tensor(
                        o1[:, db * 1024 + t2 * 512:db * 1024 + (t2 + 1) * 512],
                        ps[:, (d4 * 2 + t2) * 512:(d4 * 2 + t2 + 1) * 512],
                        1.0, boc[:, db:db + 1].broadcast_to([128, 512]),
                        ALU.mult, ALU.add)
            sem["sev"].inc(inst, 1)
        v.wait_ge(sem["ssf"].sem, SI_A)
        sem["sev"].inc(v.tensor_copy(qr[:, :], ps[0:16, 0:1024]), 1)
        v.wait_ge(sem["smsc"].sem, 1)
        chain(v, sem["svs"], v.tensor_copy(scf[:, :], ps[:, 0:512]))
        sem["sev"].inc(v.tensor_copy(scw[:, :], ps[:, 0:512]), 1)
        v.wait_ge(sem["sev"].sem, 16)
        scw3 = scw[:, :].rearrange("p (g e) -> p g e", g=8)
        for it in range(16):
            chain(v, sem["svs"],
                  v.reduce_max(m16[:, it * 8:(it + 1) * 8], scw3, axis=AX.X))
            if it == 15:
                break
            chain(v, sem["svs"],
                  v.tensor_tensor(maskf[:, :].rearrange("p (g e) -> p g e", g=8),
                                  scw3,
                                  m16[:, it * 8:(it + 1) * 8].unsqueeze(2)
                                  .broadcast_to([128, 8, 64]), ALU.is_ge))
            chain(v, sem["svs"],
                  v.scalar_tensor_tensor(scw[:, :], maskf[:, :], -1e30,
                                         scw[:, :], ALU.mult, ALU.add))
        scf3 = scf[:, :].rearrange("p (g e) -> p g e", g=8)
        chain(v, sem["svs"],
              v.tensor_tensor(maskf[:, :].rearrange("p (g e) -> p g e", g=8),
                              scf3,
                              m16[:, 120:128].unsqueeze(2).broadcast_to([128, 8, 64]),
                              ALU.is_ge))
        sem["svtk"].inc(
            v.tensor_tensor(scw[:, :].rearrange("p (g e) -> p g e", g=8),
                            scf3,
                            m16[:, 0:8].unsqueeze(2).broadcast_to([128, 8, 64]),
                            ALU.subtract), 1)
        v.wait_ge(sem["sagx"].sem, 1)
        chain(v, sem["svs"], v.tensor_mul(scf[:, :], scf[:, :], maskf[:, :]))
        chain(v, sem["svs"], v.reduce_sum(s8[:, 0:8], scf3, axis=AX.X))
        chain(v, sem["svs"], v.reciprocal(s8[:, 8:16], s8[:, 0:8]))
        sem["svgt"].inc(
            v.tensor_tensor(gate_tm[:, :].rearrange(
                                "p (g f) -> p g f", g=8)[:, :, 0:64],
                            scf3,
                            s8[:, 8:16].unsqueeze(2).broadcast_to([128, 8, 64]),
                            ALU.mult), 1)
        v.wait_ge(sem["ssf"].sem, SI_FIN)
        for n4 in range(4):
            for t2 in range(2):
                inst = v.tensor_copy(
                    zsb[:, n4 * 1024 + t2 * 512:n4 * 1024 + (t2 + 1) * 512],
                    ps[:, (n4 * 2 + t2) * 512:(n4 * 2 + t2 + 1) * 512])
        sem["sev"].inc(inst, 1)
        v.wait_ge(sem["smrep"].sem, 1)
        for n4 in range(4):
            for t2 in range(2):
                inst = v.tensor_mul(
                    zsb[:, n4 * 1024 + t2 * 512:n4 * 1024 + (t2 + 1) * 512],
                    zsb[:, n4 * 1024 + t2 * 512:n4 * 1024 + (t2 + 1) * 512],
                    ps[:, (n4 * 2 + t2) * 512:(n4 * 2 + t2 + 1) * 512])
        sem["sev"].inc(inst, 1)
        for s in range(64):
            ff = s // 4
            par = s % 2
            v.wait_ge(sem["ssf"].sem, SI_FIN + 12 * (ff + 1))
            v.wait_ge(sem[f"sxf{par}"].sem, 16 * (s // 2 + 1))
            if s >= 2:
                v.wait_ge(sem[f"sso{par}"].sem, 16 * (s // 2))
            t4 = s % 4
            chain(v, sem["svs"],
                  v.tensor_add(ysb[:, par * 512:(par + 1) * 512],
                               xf[:, par * 512:(par + 1) * 512],
                               ps[:, t4 * 512:(t4 + 1) * 512]))
            sem["sefin"].inc(
                v.tensor_add(ysb[:, par * 512:(par + 1) * 512],
                             ysb[:, par * 512:(par + 1) * 512],
                             ps[:, 2048 + t4 * 512:2048 + (t4 + 1) * 512]), 1)
    blk.vector(v_main)

    def a_main(a):
        for b in range(8):
            a.wait_ge(sem["smst"].sem, b + 1)
            a.wait_ge(sem["smot"].sem, b)
            for h in range(16):
                inst = a.activation(
                    stsb[:, h * 128:(h + 1) * 128],
                    ps[:, (h % 2) * 1024 + (h // 2) * 128:
                       (h % 2) * 1024 + (h // 2) * 128 + 128],
                    AF.Exp, bias=m12[:, 0:1])
            sem["sast"].inc(inst, 1)
        a.wait_ge(sem["svtk"].sem, 1)
        sem["sagx"].inc(a.activation(scf[:, :], scw[:, :], AF.Exp), 1)
        for s in range(64):
            ff = s // 4
            th, og, t4 = ff // 8, ff % 8, s % 4
            par = s % 2
            a.wait_ge(sem["sefin"].sem, s + 1)
            sem[f"sso{par}"].inc(
                a.dma_start(out=dram["y"][(th * 4 + t4) * 128:
                                          (th * 4 + t4 + 1) * 128,
                                          og * 512:(og + 1) * 512],
                            in_=ysb[:, par * 512:(par + 1) * 512]), 16)
    blk.scalar(a_main)

    if dbg:
        def g_dbg(g):
            g.wait_ge(sem["sev"].sem, 12)
            for n, buf in [("qh_dbg", qh), ("kh_dbg", kh), ("vh_dbg", vh)]:
                sem["sdbg"].inc(
                    g.dma_start(out=dram[n][:, :], in_=buf[:, :]), 16)
            g.wait_ge(sem["sevo"].sem, 8)
            sem["sdbg"].inc(
                g.dma_start(out=dram["oT_dbg"][:, :], in_=oT[:, :]), 16)
            g.wait_ge(sem["sev"].sem, 15)
            sem["sdbg"].inc(
                g.dma_start(out=dram["qr_dbg"][:, :], in_=qr[:, :]), 16)
            g.wait_ge(sem["svgt"].sem, 1)
            sem["sdbg"].inc(
                g.dma_start(out=dram["m16_dbg"][:, :], in_=m16[:, :]), 16)
            sem["sdbg"].inc(
                g.dma_start(out=dram["gtm_dbg"][:, :], in_=gate_tm[:, :]), 16)
            g.wait_ge(sem["sgt"].sem, 16)
            sem["sdbg"].inc(
                g.dma_start(out=dram["gfm_dbg"][:, :], in_=gate_fm[:, :]), 16)
            g.wait_ge(sem["sev"].sem, 18)
            sem["sdbg"].inc(
                g.dma_start(out=dram["zg_dbg"][:, :], in_=zsb[:, 0:4096]), 16)
        blk.gpsimd(g_dbg)

    def sy_main(sy):
        sy.wait_ge(sem["svgt"].sem, 1)
        sem["sgt"].inc(
            sy.dma_start(gate_fm[:, :].rearrange("p (tb f) -> p tb f", tb=8),
                         gate_tm[:, :].rearrange("p (tb f) -> p tb f", tb=8),
                         transpose=True), 16)
        for s in range(64):
            ff = s // 4
            th, og, t4 = ff // 8, ff % 8, s % 4
            par = s % 2
            if s >= 2:
                sy.wait_ge(sem["sefin"].sem, s - 1)
            sem[f"sxf{par}"].inc(
                sy.dma_start(out=xf[:, par * 512:(par + 1) * 512],
                             in_=dram["x1e"][1 + (th * 4 + t4) * 128:
                                             1 + (th * 4 + t4 + 1) * 128,
                                             og * 512:(og + 1) * 512]), 16)
    blk.sync(sy_main)

    def g_fin(g):
        g.wait_ge(sem["sso0"].sem, 16 * 32)
        g.wait_ge(sem["sso1"].sem, 16 * 32)
        if dbg:
            g.wait_ge(sem["sdbg"].sem, 16 * 10)
    blk.gpsimd(g_fin)


def build_phase_d(dbg=False):
    nc = bass.Bass("TRN2", target_bir_lowering=False)
    dram = {}
    ins = [("x1e", [1026, D]), ("xh_rows", [2, D]), ("xh_cols", [128, 64]),
           ("convw", [3, D, 3072]), ("convb", [128, 24]),
           ("wq", [1024, 1024]), ("wk", [1024, 1024]), ("wv", [1024, 1024]),
           ("bq", [128, 8]), ("bk", [128, 8]), ("bv_row", [1, 1024]),
           ("wo2", [1024, 1024]), ("bo", [128, 8]),
           ("projw", [1024, D]), ("projb_row", [1, D]),
           ("AT", [D, 512]), ("BT", [512, D]), ("WqT", [D, 16]),
           ("keysT", [16, 64]), ("REP", [64, 512])]
    for n, shp in ins:
        dram[n] = nc.dram_tensor(n, shp, F32, kind="ExternalInput").ap()
    dram["y"] = nc.dram_tensor("y", [T, D], F32, kind="ExternalOutput").ap()
    sems = list(SEMS_D)
    if dbg:
        for n in ["qh_dbg", "kh_dbg", "vh_dbg", "oT_dbg"]:
            dram[n] = nc.dram_tensor(n, [128, 8 * 1024], BF16,
                                     kind="ExternalOutput").ap()
        dram["qr_dbg"] = nc.dram_tensor("qr_dbg", [16, 1024], BF16,
                                        kind="ExternalOutput").ap()
        dram["m16_dbg"] = nc.dram_tensor("m16_dbg", [128, 128], F32,
                                         kind="ExternalOutput").ap()
        dram["gtm_dbg"] = nc.dram_tensor("gtm_dbg", [128, 1024], BF16,
                                         kind="ExternalOutput").ap()
        dram["gfm_dbg"] = nc.dram_tensor("gfm_dbg", [128, 1024], BF16,
                                         kind="ExternalOutput").ap()
        dram["zg_dbg"] = nc.dram_tensor("zg_dbg", [128, 4096], BF16,
                                        kind="ExternalOutput").ap()
        sems.append("sdbg")
    es = ExitStack()
    with es:
        sem = {n: Ctr(es.enter_context(nc.semaphore(n))) for n in sems}
        blk = es.enter_context(nc.Block())
        emit_phase_d(nc, es, blk, sem, dram, dbg=dbg)
    return nc


def host_inputs_phase_d(core, inp, x1_full):
    b, p = core // 4, core % 4
    t0 = p * T
    x1 = np.asarray(x1_full[b], np.float32)
    zrow = np.zeros((1, D), np.float32)
    left = x1[t0 - 1:t0] if p > 0 else zrow
    right = x1[t0 + T:t0 + T + 1] if p < 3 else zrow
    x1e = np.concatenate([left, x1[t0:t0 + T], right], 0)
    xh_rows = np.concatenate([left, right], 0)
    xh_cols = np.concatenate([left.reshape(32, 128).T,
                              right.reshape(32, 128).T], 1)
    REP = np.zeros((64, 512), np.float32)
    for nt in range(4):
        for pp in range(128):
            REP[nt * 16 + pp // 8, nt * 128 + pp] = 2.0
    d = {
        "x1e": np.ascontiguousarray(x1e),
        "xh_rows": np.ascontiguousarray(xh_rows),
        "xh_cols": np.ascontiguousarray(xh_cols),
        "convw": np.ascontiguousarray(np.asarray(inp["L1_attn_conv_w"], np.float32)),
        "convb": np.ascontiguousarray(
            np.asarray(inp["L1_attn_conv_b"], np.float32).reshape(24, 128).T),
        "wq": np.ascontiguousarray(
            np.asarray(inp["L1_attn_wq"], np.float32).reshape(1024, 1024)),
        "wk": np.ascontiguousarray(
            np.asarray(inp["L1_attn_wk"], np.float32).reshape(1024, 1024)),
        "wv": np.ascontiguousarray(
            np.asarray(inp["L1_attn_wv"], np.float32).reshape(1024, 1024)),
        "bq": np.ascontiguousarray(
            np.asarray(inp["L1_attn_bq"], np.float32).reshape(8, 128).T),
        "bk": np.ascontiguousarray(
            np.asarray(inp["L1_attn_bk"], np.float32).reshape(8, 128).T),
        "bv_row": np.ascontiguousarray(
            np.asarray(inp["L1_attn_bv"], np.float32).reshape(1, 1024)),
        "wo2": np.ascontiguousarray(
            np.asarray(inp["L1_attn_wo"], np.float32).reshape(1024, 1024)),
        "bo": np.ascontiguousarray(
            np.asarray(inp["L1_attn_bo"], np.float32).reshape(8, 128).T),
        "projw": np.ascontiguousarray(np.asarray(inp["L1_attn_proj_w"], np.float32)),
        "projb_row": np.ascontiguousarray(
            np.asarray(inp["L1_attn_proj_b"], np.float32).reshape(1, D)),
        "AT": np.ascontiguousarray(np.asarray(inp["L1_ffn_A_w"], np.float32).T),
        "BT": np.ascontiguousarray(np.asarray(inp["L1_ffn_B_w"], np.float32).T),
        "WqT": np.ascontiguousarray(np.asarray(inp["L1_ffn_Wq_w"], np.float32).T),
        "keysT": np.ascontiguousarray(np.asarray(inp["L1_ffn_keys"], np.float32).T),
        "REP": REP,
    }
    return d


def kernel(**inputs):
    inp = {k: np.asarray(v) for k, v in inputs.items()}
    cores = list(range(8))
    ncA = build_phase_a()
    imsA = [host_inputs_phase_a(c, inp) for c in cores]
    resA = run_bass_kernel_spmd(ncA, imsA, core_ids=cores).results
    ncB = build_phase_b()
    zs = np.zeros((128, 4096), ml_dtypes.bfloat16)

    def bfiles(c, s_in):
        a = resA[c]
        return {"k_fm": np.asarray(a["k_fm"]), "r_fm": np.asarray(a["r_fm"]),
                "k_tm": np.asarray(a["k_tm"]), "v_tm": np.asarray(a["v_tm"]),
                "s_in": s_in}
    imsB1 = [host_inputs_phase_b(c, inp, bfiles(c, zs)) for c in cores]
    resB1 = run_bass_kernel_spmd(ncB, imsB1, core_ids=cores).results
    sins = [zs if c % 4 == 0 else np.asarray(resB1[c - 1]["s_out"])
            for c in cores]
    imsB2 = [host_inputs_phase_b(c, inp, bfiles(c, sins[c])) for c in cores]
    resB2 = run_bass_kernel_spmd(ncB, imsB2, core_ids=cores).results
    ncC = build_phase_c()
    x = np.asarray(inp["L0_attn_hidden"])
    imsC = []
    for c in cores:
        b, p = c // 4, c % 4
        files = {"scan_out": np.asarray(resB2[c]["scan_out"]),
                 "g_fm": np.asarray(resA[c]["g_fm"]),
                 "xn_fm": np.asarray(resA[c]["xn_fm"]),
                 "x_sl": np.ascontiguousarray(x[b, p * T:(p + 1) * T])}
        imsC.append(host_inputs_phase_c(c, inp, files))
    resC = run_bass_kernel_spmd(ncC, imsC, core_ids=cores).results
    x1_full = np.zeros((2, 4096, D), np.float32)
    for c in cores:
        b, p = c // 4, c % 4
        x1_full[b, p * T:(p + 1) * T] = np.asarray(resC[c]["x1"])
    ncD = build_phase_d()
    imsD = [host_inputs_phase_d(c, inp, x1_full) for c in cores]
    resD = run_bass_kernel_spmd(ncD, imsD, core_ids=cores).results
    out = np.zeros((2, 4096, D), np.float32)
    for c in cores:
        b, p = c // 4, c % 4
        out[b, p * T:(p + 1) * T] = np.asarray(resD[c]["y"])
    return out
